# version 12
# baseline (speedup 1.0000x reference)
import numpy as np
import ml_dtypes
from contextlib import ExitStack

import concourse.bass as bass
import concourse.mybir as mybir
from concourse.bass_utils import run_bass_kernel_spmd

F32 = mybir.dt.float32
BF16 = mybir.dt.bfloat16
AF = mybir.ActivationFunctionType
AL = mybir.AluOpType
AX = mybir.AxisListType
BF = ml_dtypes.bfloat16

B, S, E, H, G, D = 4, 2048, 2048, 16, 4, 128
NE = 16
NI = 64
INTER = 8192
TOWN = 1024
SCHED = {0: [0, 1, 2, 3, 8, 9, 10, 11], 1: list(range(16))}
CSM = float(1.0 / np.sqrt(128.0))
KB = 1024


def _hadamard(n):
    h = np.array([[1.0]])
    while h.shape[0] < n:
        h = np.block([[h, h], [h, -h]])
    return h


def _bops_ordered(nchunk, by_ready):
    ops = []
    for st in range(nchunk.bit_length() - 1):
        step = 1 << st
        for c0 in range(nchunk):
            if not (c0 & step):
                blk = c0 >> (st + 1)
                ready = ((blk + 1) << (st + 1)) - 1
                ops.append((st, c0, c0 | step, ready))
    if by_ready:
        ops.sort(key=lambda t: (t[3], t[0], t[1]))
    return ops


def _slot_plan(nchunk, ops):
    slots = list(range(nchunk))
    free = [nchunk, nchunk + 1]
    plan = []
    for st, c0, c1, ready in ops:
        a, b_ = slots[c0], slots[c1]
        f0, f1 = free
        plan.append((a, b_, f0, f1, ready, st))
        slots[c0], slots[c1] = f0, f1
        free = [a, b_]
    return plan, slots


def build_nc():
    nc = bass.Bass("TRN2", target_bir_lowering=False, detect_race_conditions=False)

    dx = nc.dram_tensor("x4", [4, 512, E], F32, kind="ExternalInput").ap()
    dwq = nc.dram_tensor("wq", [H, 128, NE, 128], F32, kind="ExternalInput").ap()
    dwk = nc.dram_tensor("wk", [G, 128, NE, 128], F32, kind="ExternalInput").ap()
    dwv = nc.dram_tensor("wv", [G, 128, NE, 128], F32, kind="ExternalInput").ap()
    dbq = nc.dram_tensor("bq", [128, H], F32, kind="ExternalInput").ap()
    dbk = nc.dram_tensor("bk", [128, G], F32, kind="ExternalInput").ap()
    dbv = nc.dram_tensor("bv", [128, G], F32, kind="ExternalInput").ap()
    dwup = nc.dram_tensor("wup", [NI, 128, NE, 128], F32, kind="ExternalInput").ap()
    dwg = nc.dram_tensor("wgate", [NI, 128, NE, 128], F32, kind="ExternalInput").ap()
    dwd = nc.dram_tensor("wdown", [NE, 128, NI, 128], F32, kind="ExternalInput").ap()
    dsuug = nc.dram_tensor("suug", [128, NE], F32, kind="ExternalInput").ap()
    dsudp = nc.dram_tensor("sudp", [128, NI], F32, kind="ExternalInput").ap()
    dwsup = nc.dram_tensor("wsup", [128, NI], F32, kind="ExternalInput").ap()
    dwsg = nc.dram_tensor("wsg", [128, NI], F32, kind="ExternalInput").ap()
    dwsdp = nc.dram_tensor("wsdp", [128, NE], F32, kind="ExternalInput").ap()
    dmask = nc.dram_tensor("masks", [128, 24, 512], BF16, kind="ExternalInput").ap()
    dout = nc.dram_tensor("out", [TOWN, E], F32, kind="ExternalOutput").ap()
    dqsp = nc.dram_tensor("q_sp", [H, 2, 128, 512], BF16, kind="Internal").ap()
    dasp = nc.dram_tensor("a_sp", [H, 2, 128, 512], BF16, kind="Internal").ap()
    dlsp = nc.dram_tensor("l_sp", [32, 512], F32, kind="Internal").ap()

    h128np = _hadamard(128)
    ident_h = nc.inline_tensor(np.eye(128, dtype=BF), "identc")
    h2048_h = nc.inline_tensor((h128np / np.sqrt(2048.0)).astype(BF), "h2048c")
    h8192_h = nc.inline_tensor((h128np / np.sqrt(8192.0)).astype(BF), "h8192c")
    h8192n_h = nc.inline_tensor((-h128np / np.sqrt(8192.0)).astype(BF), "h8192nc")
    onec_h = nc.inline_tensor(np.ones((128, 1), dtype=BF), "onecc")
    epsc_h = nc.inline_tensor(np.full((128, 1), 1e-5, dtype=np.float32), "epscc")
    oner_h = nc.inline_tensor(np.ones((1, 128), dtype=np.float32), "onerc")
    identf_h = nc.inline_tensor(np.eye(128, dtype=np.float32), "identfc")

    SB_BASE = 17408
    def at(name, shape, dtype, off):
        return nc.alloc_sbuf_tensor_at(name, shape, dtype, offset=SB_BASE + off).ap()

    o = 0
    def nxt(name, shape, dtype, nbytes):
        nonlocal o
        ap_ = at(name, shape, dtype, o)
        o += nbytes
        return ap_

    ident = nxt("ident", [128, 128], BF16, 256)
    h2048 = nxt("h2048", [128, 128], BF16, 256)
    h8192 = nxt("h8192", [128, 128], BF16, 256)
    h8192n = nxt("h8192n", [128, 128], BF16, 256)
    onec = nxt("onec", [128, 1], BF16, 32)
    epsv = nxt("epsv", [128, 1], F32, 32)
    oner = nxt("oner", [1, 128], F32, 512)
    identf = nxt("identf", [128, 128], F32, 512)
    hdc = nxt("hdc", [128, NE, 128], BF16, NE * 256)
    bqs = nxt("bqs", [128, H], F32, 64)
    bks = nxt("bks", [128, G], F32, 32)
    bvs = nxt("bvs", [128, G], F32, 32)
    suugs = nxt("suugs", [128, NE], F32, 64)
    sudps = nxt("sudps", [128, NI], F32, 256)
    wsups = nxt("wsups", [128, NI], F32, 256)
    wsgs = nxt("wsgs", [128, NI], F32, 256)
    wsdps = nxt("wsdps", [128, NE], F32, 64)
    updps = nxt("updps", [128, NI], F32, 256)
    negb = nxt("negb", [128, H], F32, 64)
    qrow = nxt("qrow", [1, H, 2], F32, 128)
    krow = nxt("krow", [1, G, 4], F32, 64)
    mqr = nxt("mqr", [1, H], F32, 64)
    mkr = nxt("mkr", [1, G], F32, 32)
    mqk = nxt("mqk", [1, H], F32, 64)
    brow = nxt("brow", [1, H], F32, 64)
    nbrow = nxt("nbrow", [1, H], F32, 64)
    stats = nxt("stats", [128, 24], F32, 96)
    msq = nxt("msq", [128, 8], F32, 32)
    lnt = nxt("lnt", [128, 16], F32, 64)
    rrow = at("rrow", [1, 512], F32, o)
    hscr0_off = o
    o += 2048
    R0_END = o
    assert R0_END <= 10 * KB + 512, R0_END

    R1 = R0_END
    R2 = R1 + 24 * KB
    R3 = R2 + 16 * KB
    R4 = R3 + 8 * KB
    R5 = R4 + 16 * KB
    R6 = R5 + 32 * KB
    R7 = R6 + 32 * KB
    R8 = R7 + 32 * KB
    R9 = R8 + 16 * KB
    R10 = R9 + 16 * KB
    SBUF_END = 229376 - 17408

    masksb = at("masksb", [128, 24, 512], BF16, R1)
    xrowf = at("xrowf", [128, E], F32, R2)
    wub = [at("wub0", [128, NE, 128], BF16, R2), at("wub1", [128, NE, 128], BF16, R2 + 4 * KB)]
    wgb = [at("wgb0", [128, NE, 128], BF16, R2 + 8 * KB), at("wgb1", [128, NE, 128], BF16, R2 + 12 * KB)]
    wdb = [at("wdb0", [128, NE, 128], BF16, R2 + 8 * KB), at("wdb1", [128, NE, 128], BF16, R2 + 12 * KB)]
    xntok = [at("xntok0", [128, E], BF16, R3), at("xntok1", [128, E], BF16, R3 + 4 * KB)]
    xnf = at("xnf", [128, NE, 512], BF16, R4)
    xnown = at("xnown", [128, NE, TOWN], BF16, R5)
    obst = at("obst", [128, 4, NE, 128], BF16, R5)
    wqkvb = [at("wqkv0", [128, NE, 128], BF16, R6), at("wqkv1", [128, NE, 128], BF16, R6 + 4 * KB)]
    zsl = [at("z%d" % i, [128, TOWN], BF16, R6 + 2 * KB * i) for i in range(16)]
    obstA = at("obstA", [128, 4, NE, 128], BF16, R6)
    outs = [at("outs0", [128, E], F32, R6 + 16 * KB)] * 2
    attn_s = [at("attns0", [128, 512], BF16, R6 + 24 * KB),
              at("attns1", [128, 512], BF16, R6 + 25 * KB)]
    lread = at("lread", [32, 512], F32, R6 + 27 * KB)
    lrr = at("lrr", [32, 512], F32, R6 + 29 * KB)
    rcolB = at("rcolB", [128, 8, 32], F32, R6 + 31 * KB)
    lrow_s = [at("lrow0", [1, 512], F32, R7 + 12 * KB),
              at("lrow1", [1, 512], F32, R7 + 14 * KB)]
    zscr = at("zscr", [128, TOWN], BF16, R7)
    qstage = [at("qst0", [128, 512], BF16, R7), at("qst1", [128, 512], BF16, R7 + KB)]
    xsq = at("xsq", [128, 2048], F32, R2)
    xst = [at("xst0", [128, E], F32, R6 + 8 * KB),
           at("xst1", [128, E], F32, R6 + 16 * KB),
           at("xst2", [128, E], F32, R6 + 24 * KB)]
    vstage = at("vst", [128, 512], BF16, R7 + 2 * KB)
    qsq = at("qsq", [128, 512], BF16, R7 + 3 * KB)
    sfst = at("sfst", [128, 512], BF16, R6 + 26 * KB)
    pbuf = [at("pb_0", [128, 512], BF16, R7 + 4 * KB), at("pb_1", [128, 512], BF16, R7 + 5 * KB),
            at("pb_2", [128, 512], BF16, R7 + 2 * KB)]
    pmb = [at("pm_0", [128, 512], BF16, R7 + 6 * KB), at("pm_1", [128, 512], BF16, R7 + 7 * KB),
           at("pm_2", [128, 512], BF16, R7 + 3 * KB)]
    qd = [at("qd0", [128, 512], BF16, R7 + 8 * KB), at("qd1", [128, 512], BF16, R7 + 9 * KB)]
    asb = [at("asb0", [128, 512], BF16, R7 + 10 * KB), at("asb1", [128, 512], BF16, R7 + 11 * KB)]
    silu_s = at("sil0", [128, TOWN], BF16, R7 + 12 * KB)
    rrep = at("rrep", [128, 512], F32, R7 + 12 * KB)
    Kb = at("Kb", [128, G, S], BF16, R8)
    Vb = at("Vb", [128, G, 16, 128], BF16, R9)

    h_off = []
    h_off += [R1 + 2 * KB * i for i in range(12)]
    h_off += [R3 + 2 * KB * i for i in range(4)]
    h_off += [R4 + 2 * KB * i for i in range(8)]
    h_off += [R7 + 0 * KB, R7 + 2 * KB, R7 + 4 * KB, R7 + 6 * KB, R7 + 8 * KB, R7 + 10 * KB]
    h_off += [R7 + 16 * KB + 2 * KB * i for i in range(8)]
    h_off += [R8 + 2 * KB * i for i in range(8)]
    h_off += [R9 + 2 * KB * i for i in range(8)]
    h_off += [R5 + 16 * KB + 2 * KB * i for i in range(8)]
    n_tail = (SBUF_END - R10) // (2 * KB)
    h_off += [R10 + 2 * KB * i for i in range(n_tail)]
    assert len(h_off) >= 64, len(h_off)
    hsl = [at("h%d" % i, [128, TOWN], BF16, h_off[i]) for i in range(64)]
    hsl.append(at("h64", [128, TOWN], BF16, hscr0_off))
    hsl.append(at("h65", [128, TOWN], BF16, R7 + 14 * KB))

    pbk = [nc.alloc_psum_tensor("pbk%d" % i, [128, 512], F32).ap() for i in range(8)]

    es = ExitStack()
    SEMS = [
        "s_cst", "s_x0", "s_x1", "s_x2", "s_x3", "s_xf", "s_xd", "s_ln", "s_sq", "s_lnt", "s_lne", "s_xn",
        "s_w0", "s_w1", "s_wc", "s_ev", "s_sn", "s_snm",
        "s_qs0", "s_qs1", "s_vt", "s_vev",
        "s_b2", "s_pe", "s_b3", "s_b4", "s_prep",
        "s_zm", "s_zb",
        "s_qr0", "s_qr1", "s_sc", "s_px", "s_pm", "s_av", "s_rr", "s_rp",
        "s_ao", "s_aw0", "s_aw1",
        "s_u0", "s_u1", "s_g0", "s_g1", "s_uc", "s_gc", "s_sl", "s_h",
        "s_m5", "s_m5e", "s_h4",
        "s_d0", "s_d1", "s_dc", "s_sf", "s_ft",
        "s_ar0", "s_ar1",
        "s_oo", "s_ow", "s_vg", "s_lw", "s_lr",
    ]
    sems = {n: es.enter_context(nc.semaphore(n)) for n in SEMS}
    sm = type("S", (), sems)
    block = es.enter_context(nc.Block())

    n_cst = 17
    def _kv(ts):
        return [("k", g, ts) for g in range(G)] + [("v", g, ts) for g in range(G)]
    wstream = (_kv(0) + [("q", h, 0) for h in range(12)]
               + _kv(1) + [("q", h, 0) for h in range(12, 16)] + [("q", h, 1) for h in range(8)]
               + _kv(2) + [("q", h, 1) for h in range(8, 16)]
               + _kv(3))
    NW = len(wstream)
    _release = [4 * ts + 4 for (_k, _i, ts) in wstream]
    POPS = 5
    _wpa = {}
    _wp = 0
    for _bi in range(16):
        _ne = 0
        while _ne < NW and _release[_ne] <= _bi:
            _ne += 1
        _wp = min(_wp + POPS, _ne)
        _wpa[_bi] = _wp
    sq_point = {bi: _wpa[bi] - 1 for bi in range(16)}
    stat_drains = [n for n in range(NW)
                   if n > 0 and wstream[n][0] == "k" and wstream[n][2] >= 1
                   and wstream[n - 1][2] != wstream[n][2]]
    _kqpos = [n for n in range(NW) if wstream[n][0] != "v"]
    lnt_pre = {n: [] for n in range(NW)}
    lnt_post = {n: [] for n in range(NW)}
    for bi in range(16):
        W = _wpa[bi] if (bi + 1) % 4 == 0 else _wpa[min(bi + 1, 15)]
        lastkq = max((q for q in _kqpos if q < W), default=-1)
        if lastkq < 0:
            lnt_pre.setdefault(-1, []).append(bi)
            continue
        nxt = next((q for q in _kqpos if q > lastkq), NW)
        d = next((d_ for d_ in stat_drains if lastkq < d_ <= nxt), None)
        if d is not None:
            lnt_pre[d].append(bi)
        else:
            lnt_post[nxt].append(bi)
    def _Wf(j):
        return _wpa[j] if (j + 1) % 4 == 0 else _wpa[min(j + 1, 15)]
    xq_need = {}
    for bi in range(16):
        j = bi - 3
        W = _Wf(j) if j >= 0 else 0
        ms = [wstream[p][1] + 16 * wstream[p][2] for p in range(W) if wstream[p][0] == "q"]
        xq_need[bi] = (max(ms) - 2) if ms else -1
    xspill_trig = {}
    for bi in range(16):
        if xq_need[bi] >= 0:
            xspill_trig.setdefault(xq_need[bi], []).append(bi)
    pos_q = {}
    vi_of = {}
    stat_of = {}
    nvv = 0
    nst = 0
    for n, (kind, idx, ts) in enumerate(wstream):
        if kind == "v":
            vi_of[n] = nvv
            nvv += 1
        else:
            stat_of[n] = nst
            nst += 1
            if kind == "q":
                pos_q[ts * 16 + idx] = n
    NSTAT = nst

    jobs3 = []
    for tau in (0, 1):
        for h in range(H):
            for js, sl in enumerate(SCHED[tau]):
                jobs3.append((tau, h, js, sl, tau * 16 + h))
    NJ3 = len(jobs3)
    grp_last = {}
    for j, (tau, h, js, sl, n) in enumerate(jobs3):
        grp_last[n] = j

    hops = [op for op in _bops_ordered(64, by_ready=True) if op[0] < 5]
    hplan, hfin = _slot_plan(64, hops)
    hperm = [hfin[c] for c in range(64)]
    ord4 = {}
    _c4 = 0
    for _st, _c0, _c1, _r in hops:
        if _st == 4:
            _c4 += 1
            ord4[_c0] = ord4[_c1] = _c4
    p5pairs = sorted(range(32), key=lambda c0: (max(ord4[c0], ord4[c0 + 32]), c0))
    NQ4 = NE * 4

    @block.sync
    def _(sync):
        sync.dma_start(out=epsv[:, :], in_=epsc_h.ap()[:, :]).then_inc(sm.s_cst, 16)
        sync.dma_start(out=ident[:, :], in_=ident_h.ap()[:, :]).then_inc(sm.s_cst, 16)
        sync.dma_start(out=h2048[:, :], in_=h2048_h.ap()[:, :]).then_inc(sm.s_cst, 16)
        sync.dma_start(out=h8192[:, :], in_=h8192_h.ap()[:, :]).then_inc(sm.s_cst, 16)
        sync.dma_start(out=h8192n[:, :], in_=h8192n_h.ap()[:, :]).then_inc(sm.s_cst, 16)
        sync.dma_start(out=onec[:, :], in_=onec_h.ap()[:, :]).then_inc(sm.s_cst, 16)
        sync.dma_start(out=oner[:, :], in_=oner_h.ap()[:, :]).then_inc(sm.s_cst, 16)
        sync.dma_start(out=identf[:, :], in_=identf_h.ap()[:, :]).then_inc(sm.s_cst, 16)
        sync.dma_start(out=masksb[:, :, :], in_=dmask[:, :, :]).then_inc(sm.s_cst, 16)
        for dst, src in [(bqs, dbq), (bks, dbk), (bvs, dbv), (suugs, dsuug),
                         (sudps, dsudp), (wsups, dwsup), (wsgs, dwsg), (wsdps, dwsdp)]:
            sync.dma_start(out=dst[:, :], in_=src[:, :]).then_inc(sm.s_cst, 16)
        def xload(bi):
            p = bi % 3
            if bi >= 3:
                sync.wait_ge(sm.s_xd, bi - 2)
            sync.dma_start(
                out=xst[p][:, :], in_=dx[bi // 4, (bi % 4) * 128:(bi % 4) * 128 + 128, :]
            ).then_inc([sm.s_x0, sm.s_x1, sm.s_x2][p], 16)

        for bi in range(16):
            if xq_need[bi] < 0:
                xload(bi)
        for m in range(32):
            p = m % 2
            sync.wait_ge(sm.s_ev, pos_q[m] + 1)
            sync.dma_start(out=dqsp[m % 16, m // 16, :, :], in_=qstage[p][:, :]).then_inc(
                [sm.s_qs0, sm.s_qs1][p], 16)
            for bi in xspill_trig.get(m, []):
                xload(bi)
        for n in range(34):
            if n < 32:
                p = n % 2
                sync.wait_ge([sm.s_qs0, sm.s_qs1][p], 16 * (n // 2 + 1))
                if n >= 2:
                    sync.wait_ge(sm.s_ao, n - 1)
                sync.dma_start(out=qd[p][:, :], in_=dqsp[n % 16, n // 16, :, :]).then_inc(
                    [sm.s_qr0, sm.s_qr1][p], 16)
            if n >= 2:
                m = n - 2
                p = m % 2
                sync.wait_ge(sm.s_ao, m + 1)
                sync.dma_start(out=dasp[m % 16, m // 16, :, :], in_=asb[p][:, :]).then_inc(
                    [sm.s_aw0, sm.s_aw1][p], 16)
                sync.dma_start(out=dlsp[m:m + 1, :], in_=lrow_s[p][0:1, :]).then_inc(
                    sm.s_lw, 16)
        sync.wait_ge(sm.s_uc, NI)
        sync.wait_ge(sm.s_gc, NI)
        sync.dma_start(out=lread[:, :], in_=dlsp[:, :]).then_inc(sm.s_lr, 16)
        for tau in range(2):
            for sb in range(4):
                qb = tau * 4 + sb
                sync.wait_ge(sm.s_xd, 16 + qb)
                sync.dma_start(
                    out=xrowf[:, :], in_=dx[tau, sb * 128:(sb + 1) * 128, :]
                ).then_inc(sm.s_xf, 16)
                sync.wait_ge(sm.s_oo, qb + 1)
                sync.dma_start(out=dout[qb * 128:(qb + 1) * 128, :], in_=outs[qb % 2][:, :]).then_inc(
                    sm.s_ow, 16)
        sync.wait_ge(sm.s_ow, 16 * 8)

    @block.gpsimd
    def _(g):
        for n, (kind, idx, ts) in enumerate(wstream):
            p = n % 2
            if n >= 2:
                g.wait_ge(sm.s_wc, n - 1)
            src = {"k": dwk, "v": dwv, "q": dwq}[kind]
            g.dma_start(out=wqkvb[p][:, :, :], in_=src[idx, :, :, :]).then_inc(
                [sm.s_w0, sm.s_w1][p], 16)
        g.wait_ge(sm.s_ln, 16)
        for ic in range(NI):
            p = ic % 2
            if ic >= 2:
                g.wait_ge(sm.s_uc, ic - 1)
            g.dma_start(out=wub[p][:, :, :], in_=dwup[ic, :, :, :]).then_inc(
                [sm.s_u0, sm.s_u1][p], 16)
            if ic >= 2:
                g.wait_ge(sm.s_gc, ic - 1)
            g.dma_start(out=wgb[p][:, :, :], in_=dwg[ic, :, :, :]).then_inc(
                [sm.s_g0, sm.s_g1][p], 16)
        g.wait_ge(sm.s_gc, NI)
        for tau in range(2):
            for u in range(NQ4):
                n = tau * NQ4 + u
                ec, q4 = u // 4, u % 4
                p = n % 2
                if n >= 2:
                    g.wait_ge(sm.s_dc, n - 1)
                g.dma_start(
                    out=wdb[p][:, :, :], in_=dwd[ec, :, q4 * 16:(q4 + 1) * 16, :]
                ).then_inc([sm.s_d0, sm.s_d1][p], 16)

    @block.vector
    def _(v):
        _vgc = [0]

        def vg(inst):
            _vgc[0] += 1
            inst.then_inc(sm.s_vg, 1)
            v.wait_ge(sm.s_vg, _vgc[0])

        _si = [0]

        def emit_norm(v, n, kind, idx, ts):
            if kind == "v":
                gi = vi_of[n]
                v.wait_ge(sm.s_vt, gi + 1)
                v.tensor_copy(
                    Vb[:, idx, ts * 4:ts * 4 + 4, :],
                    pbk[4 + gi % 2].bitcast(BF16)[:, 0:512].rearrange("p (s d) -> p s d", s=4),
                ).then_inc(sm.s_vev, 1)
                return
            si = _si[0]
            v.wait_ge(sm.s_ev, n + 1)
            if si > 0:
                v.wait_ge(sm.s_snm, si)
            if kind == "k":
                kt = Kb[:, idx, ts * 512:(ts + 1) * 512]
                v.tensor_mul(qsq[:, :], kt, kt).then_inc(sm.s_sn, 1)
                v.wait_ge(sm.s_snm, si + 1)
                v.reduce_max(krow[0:1, idx, ts:ts + 1], pbk[6][0:1, :], axis=AX.X).then_inc(sm.s_sn, 1)
            else:
                m = ts * 16 + idx
                qt = qstage[m % 2]
                v.tensor_mul(qsq[:, :], qt[:, :], qt[:, :]).then_inc(sm.s_sn, 1)
                v.wait_ge(sm.s_snm, si + 1)
                v.reduce_max(qrow[0:1, idx, ts:ts + 1], pbk[6][0:1, :], axis=AX.X).then_inc(sm.s_sn, 1)
            _si[0] = si + 1
        wpos = 0

        def lc(bi, k):
            b2 = 8 * (bi % 2)
            return lnt[:, b2 + k:b2 + k + 1]

        def ln_stats(bi):
            p = bi % 3
            v.wait_ge([sm.s_x0, sm.s_x1, sm.s_x2][p], 16 * (bi // 3 + 1))
            v.tensor_mul(xsq[:, :], xst[p][:, :], xst[p][:, :])
            v.reduce_sum(lc(bi, 0), xst[p][:, :], axis=AX.X)
            vg(v.reduce_sum(lc(bi, 1), xsq[:, :], axis=AX.X))
            v.tensor_scalar_mul(lc(bi, 3), lc(bi, 0), 1.0 / 2048.0)
            vg(v.tensor_scalar_mul(lc(bi, 4), lc(bi, 1), 1.0 / 2048.0))
            vg(v.tensor_mul(lc(bi, 5), lc(bi, 3), lc(bi, 3)))
            v.tensor_sub(lc(bi, 4), lc(bi, 4), lc(bi, 5)).then_inc(sm.s_ln, 1)

        def ln_finish(bi):
            p = bi % 3
            ts = bi // 4
            v.wait_ge(sm.s_sq, bi + 1)
            vg(v.reciprocal(lc(bi, 6), lc(bi, 7)))
            vg(v.tensor_mul(lc(bi, 5), lc(bi, 3), lc(bi, 6)))
            v.tensor_scalar(
                xntok[bi % 2][:, :], xst[p][:, :], lc(bi, 6), lc(bi, 5),
                AL.mult, AL.subtract,
            ).then_inc(sm.s_xd, 1)
            v.wait_ge(sm.s_lnt, 2 * bi + 2)
            if bi == 12:
                v.wait_ge(sm.s_wc, 48)
            for half in range(2):
                srcv = pbk[half].bitcast(BF16)[:, 0:1024]
                if ts < 2:
                    dstv = xnown[:, 8 * half:8 * half + 8,
                                 ts * 512 + (bi % 4) * 128: ts * 512 + (bi % 4) * 128 + 128]
                else:
                    dstv = xnf[:, 8 * half:8 * half + 8, (bi % 4) * 128:(bi % 4) * 128 + 128]
                v.tensor_copy(dstv, srcv.rearrange("p (c t) -> p c t", c=8)).then_inc(sm.s_lne, 1)
            v.engine_nop().then_inc(sm.s_xn, 1)

        norm_end = 0

        def pop_norms(k):
            nonlocal wpos
            while k > 0 and wpos < norm_end:
                n = wpos
                kind, idx, _ts = wstream[n]
                emit_norm(v, n, kind, idx, _ts)
                wpos += 1
                k -= 1

        for bi in range(16):
            if bi % 4 == 0 and bi >= 1:
                ln_finish(bi - 1)
                if bi == 4:
                    v.wait_ge(sm.s_cst, 16 * n_cst)
                    for c in range(NE):
                        v.tensor_scalar_mul(hdc[:, c, :], h2048[:, :], suugs[:, c:c + 1])
                    v.tensor_mul(updps[:, :], wsups[:, :], sudps[:, :]).then_inc(sm.s_prep, 1)
            norm_end = 0
            while norm_end < NW and _release[norm_end] <= bi:
                norm_end += 1
            pop_norms(POPS)
            ln_stats(bi)
            if bi % 4 != 0 and bi >= 1:
                ln_finish(bi - 1)
        ln_finish(15)
        norm_end = NW
        pop_norms(NW)
        v.engine_nop()
        vg(v.engine_nop())
        v.reduce_max(mqr[0:1, :], qrow[0:1, :, :], axis=AX.X)
        vg(v.reduce_max(mkr[0:1, :], krow[0:1, :, :], axis=AX.X))
        v.tensor_mul(
            mqk[0:1, :].rearrange("o (g r) -> o g r", g=G),
            mqr[0:1, :].rearrange("o (g r) -> o g r", g=G),
            mkr[0:1, :].unsqueeze(2).broadcast_to([1, G, 4]),
        ).then_inc(sm.s_b2, 1)
        v.wait_ge(sm.s_pe, 1)
        v.tensor_scalar_mul(nbrow[0:1, :], brow[0:1, :], -1.0).then_inc(sm.s_b3, 1)
        v.wait_ge(sm.s_b4, 1)
        v.tensor_copy(negb[:, :], pbk[6][:, 0:H]).then_inc(sm.s_b4, 1)
        for c in range(NE):
            v.wait_ge(sm.s_zm, c + 1)
            bk_ = c % 4
            v.tensor_copy(zsl[c][:, 0:512], pbk[bk_][:, :])
            v.tensor_copy(zsl[c][:, 512:1024], pbk[4 + bk_][:, :]).then_inc(sm.s_zb, 1)
        zops = [(c0, c0 | (1 << st)) for st in range(4) for c0 in range(16) if not (c0 & (1 << st))]
        fin_pend = []

        def fin_emit(n, jl):
            v.wait_ge(sm.s_av, jl + 1)
            if n >= 2:
                v.wait_ge([sm.s_aw0, sm.s_aw1][n % 2], 16 * ((n - 2) // 2 + 1))
                v.wait_ge(sm.s_lw, 16 * (n - 1))
            v.tensor_copy(lrow_s[n % 2][0:1, :], pbk[4 + n % 2][0:1, :])
            v.tensor_copy(asb[n % 2][:, :], pbk[2 + n % 2][:, :]).then_inc(sm.s_ao, 1)
            if zops and n >= 2:
                c0, c1 = zops.pop(0)
                v.tensor_sub(zscr[:, :], zsl[c0][:, :], zsl[c1][:, :])
                v.tensor_add(zsl[c0][:, :], zsl[c0][:, :], zsl[c1][:, :])
                v.tensor_copy(zsl[c1][:, :], zscr[:, :])

        for j, (tau, h, js, sl, n) in enumerate(jobs3):
            p = j % 3
            mi = (8 + js) if tau else js
            v.wait_ge(sm.s_px, j + 1)
            if j >= 3:
                v.wait_ge(sm.s_av, j - 2)
            v.tensor_mul(pmb[p][:, :], pbuf[p][:, :], masksb[:, mi, :]).then_inc(sm.s_pm, 1)
            if j == grp_last[n]:
                fin_pend.append((j + 3, n, j))
            while fin_pend and fin_pend[0][0] <= j:
                _, n_, jl_ = fin_pend.pop(0)
                fin_emit(n_, jl_)
        for _, n_, jl_ in fin_pend:
            fin_emit(n_, jl_)
        for c0, c1 in zops:
            v.tensor_sub(zscr[:, :], zsl[c0][:, :], zsl[c1][:, :])
            v.tensor_add(zsl[c0][:, :], zsl[c0][:, :], zsl[c1][:, :])
            v.tensor_copy(zsl[c1][:, :], zscr[:, :])
        v.engine_nop().then_inc(sm.s_prep, 1)
        v.wait_ge(sm.s_aw0, 16 * 16)
        v.wait_ge(sm.s_aw1, 16 * 16)
        v.wait_ge(sm.s_qs0, 16 * 16)
        v.wait_ge(sm.s_qs1, 16 * 16)
        v.wait_ge(sm.s_lw, 16 * 32)
        hq = list(hplan)
        for ic in range(NI):
            bu = [pbk[0], pbk[1]] if ic % 2 == 0 else [pbk[4], pbk[5]]
            v.wait_ge(sm.s_uc, ic + 1)
            v.wait_ge(sm.s_sl, 2 * (ic + 1))
            for tau in range(2):
                v.scalar_tensor_tensor(
                    hsl[ic][:, tau * 512:(tau + 1) * 512],
                    bu[tau][:, :], updps[:, ic:ic + 1],
                    silu_s[:, tau * 512:(tau + 1) * 512],
                    AL.mult, AL.mult,
                )
            v.engine_nop().then_inc(sm.s_h, 1)
            while hq and hq[0][4] <= ic:
                a, b_, f0, f1, _r, st_ = hq.pop(0)
                v.tensor_add(hsl[f0][:, :], hsl[a][:, :], hsl[b_][:, :])
                inst = v.tensor_sub(hsl[f1][:, :], hsl[a][:, :], hsl[b_][:, :])
                if st_ == 4:
                    inst.then_inc(sm.s_h4, 1)
        assert not hq
        for k, c0 in enumerate(p5pairs):
            pp = k % 2
            v.wait_ge(sm.s_m5, k + 1)
            v.tensor_copy(hsl[hperm[c0]][:, 0:512], pbk[4 * pp][:, :])
            v.tensor_copy(hsl[hperm[c0]][:, 512:1024], pbk[4 * pp + 1][:, :]).then_inc(sm.s_m5e, 1)
            v.tensor_copy(hsl[hperm[c0 + 32]][:, 0:512], pbk[4 * pp + 2][:, :])
            v.tensor_copy(hsl[hperm[c0 + 32]][:, 512:1024], pbk[4 * pp + 3][:, :]).then_inc(sm.s_m5e, 1)
        v.wait_ge(sm.s_lr, 16)
        v.reciprocal(lrr[:, :], lread[:, :]).then_inc(sm.s_lw, 1)
        for qb in range(8):
            v.wait_ge(sm.s_ft, qb + 1)
            v.tensor_copy(rcolB[:, qb, :], pbk[6 + qb % 2][:, 0:32]).then_inc(sm.s_sf, 1)
        for tau in range(2):
            for ec in range(NE):
                n = tau * 16 + ec
                v.wait_ge(sm.s_dc, tau * NQ4 + (ec + 1) * 4)
                if n >= 1:
                    v.wait_ge(sm.s_ft, 8 + 2 * n)
                v.tensor_scalar_mul(sfst[:, :], pbk[ec % 4][:, :], wsdps[:, ec:ec + 1]).then_inc(sm.s_sf, 1)
                v.wait_ge(sm.s_ft, 8 + 2 * n + 1)
                v.tensor_copy(
                    obst[:, :, ec, :],
                    pbk[4 + n % 2].bitcast(BF16)[:, 0:512].rearrange("p (s d) -> p s d", s=4),
                ).then_inc(sm.s_sf, 1)
                v.wait_ge(sm.s_ft, 8 + 2 * n + 2)
                v.tensor_copy(
                    obstA[:, :, ec, :],
                    pbk[6 + n % 2].bitcast(BF16)[:, 0:512].rearrange("p (s d) -> p s d", s=4),
                ).then_inc(sm.s_sf, 1)
            vg(v.engine_nop())
            for sb in range(4):
                qb = tau * 4 + sb
                v.wait_ge(sm.s_xf, 16 * (qb + 1))
                if qb >= 1:
                    v.wait_ge(sm.s_ow, 16 * qb)
                for h_ in range(H):
                    inst = v.scalar_tensor_tensor(
                        outs[qb % 2][:, h_ * 128:(h_ + 1) * 128],
                        obstA[:, sb, h_, :],
                        rcolB[:, qb, tau * 16 + h_:tau * 16 + h_ + 1],
                        obst[:, sb, h_, :],
                        AL.mult, AL.add,
                    )
                vg(inst)
                v.tensor_add(outs[qb % 2][:, :], outs[qb % 2][:, :], xrowf[:, :]).then_inc(sm.s_oo, 1)
                v.engine_nop().then_inc(sm.s_xd, 1)

    @block.scalar
    def _(s):
        s.wait_ge(sm.s_cst, 16)
        nv = 0
        swpos = 0
        first_ev = [True]
        def sqrt_of(bi):
            b2 = 8 * (bi % 2)
            s.wait_ge(sm.s_ln, bi + 1)
            s.activation(lnt[:, b2 + 7:b2 + 8], lnt[:, b2 + 4:b2 + 5], AF.Sqrt,
                         bias=epsv[:, 0:1], scale=1.0).then_inc(sm.s_sq, 1)

        from collections import defaultdict
        sq_trig = defaultdict(list)
        for bi in range(16):
            if sq_point[bi] < 0:
                sqrt_of(bi)
            else:
                sq_trig[sq_point[bi]].append(bi)
        for n, (kind, idx, ts) in enumerate(wstream):
            if True:
                bank = pbk[2 + n % 2]
                if first_ev[0]:
                    s.wait_ge(sm.s_cst, 16 * n_cst)
                    first_ev[0] = False
                s.wait_ge(sm.s_wc, n + 1)
                if kind == "k":
                    s.activation(Kb[:, idx, ts * 512:(ts + 1) * 512], bank[:, :], AF.Identity,
                                 bias=bks[:, idx:idx + 1]).then_inc(sm.s_ev, 1)
                elif kind == "v":
                    nv += 1
                    if nv > 1:
                        s.wait_ge(sm.s_vt, nv - 1)
                    s.activation(vstage[:, :], bank[:, :], AF.Identity,
                                 bias=bvs[:, idx:idx + 1]).then_inc(sm.s_ev, 1)
                else:
                    m = ts * 16 + idx
                    p = m % 2
                    if m >= 2:
                        s.wait_ge([sm.s_qs0, sm.s_qs1][p], 16 * ((m - 2) // 2 + 1))
                        s.wait_ge(sm.s_sn, 2 * stat_of[pos_q[m - 2]] + 1)
                    s.activation(qstage[p][:, :], bank[:, :], AF.Identity,
                                 bias=bqs[:, idx:idx + 1]).then_inc(sm.s_ev, 1)
                for bi in sq_trig[n]:
                    sqrt_of(bi)
        s.wait_ge(sm.s_b2, 1)
        s.activation(brow[0:1, :], mqk[0:1, :], AF.Sqrt, scale=1.0 / 128.0).then_inc(sm.s_pe, 1)
        s.wait_ge(sm.s_b4, 2)
        sb3s = [pbk[0], pbk[1], pbk[6], pbk[7]]
        for j, (tau, h, js, sl, n) in enumerate(jobs3):
            p = j % 3
            s.wait_ge(sm.s_sc, j + 1)
            if j >= 3:
                s.wait_ge(sm.s_pm, j - 2)
            s.activation(pbuf[p][:, :], sb3s[j % 4][:, :], AF.Exp,
                         bias=negb[:, h:h + 1], scale=CSM).then_inc(sm.s_px, 1)
        s.wait_ge(sm.s_lw, 16 * 32)
        for ic in range(NI):
            bg = [pbk[2], pbk[3]] if ic % 2 == 0 else [pbk[6], pbk[7]]
            s.wait_ge(sm.s_gc, ic + 1)
            if ic >= 1:
                s.wait_ge(sm.s_h, ic)
            for tau in range(2):
                s.activation(silu_s[:, tau * 512:(tau + 1) * 512], bg[tau][:, :],
                             AF.Silu, scale=wsgs[:, ic:ic + 1]).then_inc(sm.s_sl, 1)
        for tau in range(2):
            for ec in range(NE):
                n = tau * 16 + ec
                p = n % 2
                s.wait_ge([sm.s_aw0, sm.s_aw1][p], 16 * (n // 2 + 1))
                if n >= 2:
                    s.wait_ge(sm.s_ft, 8 + 2 * (n - 2) + 2)
                s.dma_start(out=attn_s[p][:, :], in_=dasp[ec, tau, :, :]).then_inc(
                    [sm.s_ar0, sm.s_ar1][p], 16)

    @block.tensor
    def _(t):
        t.wait_ge(sm.s_cst, 32)
        first_proj = [True]
        from collections import deque
        statq = deque()
        nv = 0
        wpos = 0
        def lnt_of(bi):
            p = bi % 2
            t.wait_ge(sm.s_xd, bi + 1)
            if bi >= 1:
                t.wait_ge(sm.s_lne, 2 * bi)
            for half in range(2):
                dstb = pbk[half].bitcast(BF16)
                for i in range(8):
                    ec = half * 8 + i
                    inst = t.transpose(dstb[:, i * 128:(i + 1) * 128],
                                       xntok[p][:, ec * 128:(ec + 1) * 128], ident[:, :])
                inst.then_inc(sm.s_lnt, 1)

        def stat_drain():
            while statq:
                sj = statq.popleft()
                t.wait_ge(sm.s_sn, 2 * sj + 1)
                t.matmul(pbk[6][0:1, :], onec[:, :], qsq[:, :], start=True,
                         stop=True).then_inc(sm.s_snm, 1)

        for bi in lnt_pre.get(-1, []):
            lnt_of(bi)
        for n, (kind, idx, ts) in enumerate(wstream):
            if True:
                if n in stat_drains:
                    stat_drain()
                for bi in lnt_pre[n]:
                    lnt_of(bi)
                p = n % 2
                bank = pbk[2 + n % 2]
                if first_proj[0]:
                    t.wait_ge(sm.s_cst, 16 * n_cst)
                    first_proj[0] = False
                t.wait_ge([sm.s_w0, sm.s_w1][p], 16 * (n // 2 + 1))
                t.wait_ge(sm.s_xn, 4 * (ts + 1))
                if n >= 2:
                    t.wait_ge(sm.s_ev, n - 1)
                src = xnown[:, :, ts * 512:(ts + 1) * 512] if ts < 2 else xnf[:, :, :]
                for ec in range(NE):
                    inst = t.matmul(bank[:, :], wqkvb[p][:, ec, :], src[:, ec, :],
                                    start=(ec == 0), stop=(ec == NE - 1))
                inst.then_inc(sm.s_wc, 1)
                if statq:
                    sj = statq.popleft()
                    t.wait_ge(sm.s_sn, 2 * sj + 1)
                    t.matmul(pbk[6][0:1, :], onec[:, :], qsq[:, :], start=True, stop=True).then_inc(sm.s_snm, 1)
                if kind != "v":
                    statq.append(stat_of[n])
                else:
                    nv += 1
                    t.wait_ge(sm.s_ev, n + 1)
                    if nv >= 3:
                        t.wait_ge(sm.s_vev, nv - 2)
                    dstb = pbk[4 + (nv - 1) % 2].bitcast(BF16)
                    for i in range(4):
                        inst = t.transpose(dstb[:, i * 128:(i + 1) * 128],
                                           vstage[:, i * 128:(i + 1) * 128], ident[:, :])
                    inst.then_inc(sm.s_vt, 1)
                for bi in lnt_post[n]:
                    lnt_of(bi)
        stat_drain()
        t.wait_ge(sm.s_b3, 1)
        t.matmul(pbk[6][:, 0:H], oner[0:1, :], nbrow[0:1, :], start=True, stop=True).then_inc(sm.s_b4, 1)
        t.wait_ge(sm.s_prep, 1)
        t.wait_ge(sm.s_xn, 8)
        t.wait_ge(sm.s_lne, 32)
        t.wait_ge(sm.s_b4, 2)
        t.wait_ge(sm.s_vev, 16)
        t.wait_ge(sm.s_sn, 2 * NSTAT)
        for c in range(NE):
            bk_ = c % 4
            if c >= 4:
                t.wait_ge(sm.s_zb, c - 3)
            t.matmul(pbk[bk_][:, :], hdc[:, c, :], xnown[:, c, 0:512], start=True, stop=True)
            t.matmul(pbk[4 + bk_][:, :], hdc[:, c, :], xnown[:, c, 512:1024],
                     start=True, stop=True).then_inc(sm.s_zm, 1)
        LAG = 3
        sb3 = [pbk[0], pbk[1], pbk[6], pbk[7]]
        t.wait_ge(sm.s_zb, 16)
        t.wait_ge(sm.s_b4, 2)
        for j in range(NJ3 + LAG):
            if j < NJ3:
                tau, h, js, sl, n = jobs3[j]
                if js == 0:
                    t.wait_ge([sm.s_qr0, sm.s_qr1][n % 2], 16 * (n // 2 + 1))
                if j >= 4:
                    t.wait_ge(sm.s_px, j - 3)
                t.matmul(sb3[j % 4][:, :], Kb[:, h // 4, sl * 128:(sl + 1) * 128], qd[n % 2][:, :],
                         start=True, stop=True).then_inc(sm.s_sc, 1)
            if j >= LAG:
                jj = j - LAG
                tau, h, js, sl, n = jobs3[jj]
                p = jj % 3
                t.wait_ge(sm.s_pm, jj + 1)
                first = js == 0
                last = jj == grp_last[n]
                if first and n >= 2:
                    t.wait_ge(sm.s_ao, n - 1)
                t.matmul(pbk[2 + n % 2][:, :], Vb[:, h // 4, sl, :], pmb[p][:, :],
                         start=first, stop=last)
                t.matmul(pbk[4 + n % 2][0:1, :], onec[:, :], pmb[p][:, :],
                         start=first, stop=last).then_inc(sm.s_av, 1)
        t.wait_ge(sm.s_px, NJ3)
        t.wait_ge(sm.s_ao, 32)
        t.wait_ge(sm.s_prep, 2)
        for ic in range(NI):
            p = ic % 2
            bu = [pbk[0], pbk[1]] if ic % 2 == 0 else [pbk[4], pbk[5]]
            bg = [pbk[2], pbk[3]] if ic % 2 == 0 else [pbk[6], pbk[7]]
            t.wait_ge([sm.s_u0, sm.s_u1][p], 16 * (ic // 2 + 1))
            if ic >= 2:
                t.wait_ge(sm.s_h, ic - 1)
            for ec in range(NE):
                for tau in range(2):
                    inst = t.matmul(bu[tau][:, :], wub[p][:, ec, :],
                                    zsl[ec][:, tau * 512:(tau + 1) * 512],
                                    start=(ec == 0), stop=(ec == NE - 1))
            inst.then_inc(sm.s_uc, 1)
            t.wait_ge([sm.s_g0, sm.s_g1][p], 16 * (ic // 2 + 1))
            if ic >= 2:
                t.wait_ge(sm.s_sl, 2 * (ic - 1))
            for ec in range(NE):
                for tau in range(2):
                    inst = t.matmul(bg[tau][:, :], wgb[p][:, ec, :],
                                    zsl[ec][:, tau * 512:(tau + 1) * 512],
                                    start=(ec == 0), stop=(ec == NE - 1))
            inst.then_inc(sm.s_gc, 1)
        t.wait_ge(sm.s_sl, 2 * NI)
        for k, c0 in enumerate(p5pairs):
            pp = k % 2
            t.wait_ge(sm.s_h4, max(ord4[c0], ord4[c0 + 32]))
            if k >= 2:
                t.wait_ge(sm.s_m5e, 2 * (k - 2) + 2)
            sA, sB = hperm[c0], hperm[c0 + 32]
            for half in range(2):
                hs = slice(half * 512, half * 512 + 512)
                t.matmul(pbk[4 * pp + half][:, :], h8192[:, :], hsl[sA][:, hs],
                         start=True, stop=False)
                t.matmul(pbk[4 * pp + half][:, :], h8192[:, :], hsl[sB][:, hs],
                         start=False, stop=True)
            for half in range(2):
                hs = slice(half * 512, half * 512 + 512)
                t.matmul(pbk[4 * pp + 2 + half][:, :], h8192[:, :], hsl[sA][:, hs],
                         start=True, stop=False)
                inst = t.matmul(pbk[4 * pp + 2 + half][:, :], h8192n[:, :], hsl[sB][:, hs],
                                start=False, stop=True)
            inst.then_inc(sm.s_m5, 1)
        t.wait_ge(sm.s_m5e, NI)
        t.wait_ge(sm.s_lw, 16 * 32 + 1)
        for qb in range(8):
            if qb >= 2:
                t.wait_ge(sm.s_sf, qb - 1)
            t.transpose(pbk[6 + qb % 2][:, 0:32],
                        lrr[:, (qb % 4) * 128:(qb % 4) * 128 + 128],
                        identf[0:32, 0:32]).then_inc(sm.s_ft, 1)
        def unit_transposes(n):
            t.wait_ge(sm.s_sf, 8 + 3 * n + 1)
            t.wait_ge(sm.s_sf, 8 + 3 * (n - 2) + 2 if n >= 2 else 8)
            dstb = pbk[4 + n % 2].bitcast(BF16)
            for i in range(4):
                inst = t.transpose(dstb[:, i * 128:(i + 1) * 128],
                                   sfst[:, i * 128:(i + 1) * 128], ident[:, :])
            inst.then_inc(sm.s_ft, 1)
            t.wait_ge([sm.s_ar0, sm.s_ar1][n % 2], 16 * (n // 2 + 1))
            t.wait_ge(sm.s_sf, 8 + 3 * (n - 2) + 3 if n >= 2 else 8)
            dstb = pbk[6 + n % 2].bitcast(BF16)
            for i in range(4):
                inst = t.transpose(dstb[:, i * 128:(i + 1) * 128],
                                   attn_s[n % 2][:, i * 128:(i + 1) * 128], ident[:, :])
            inst.then_inc(sm.s_ft, 1)

        for tau in range(2):
            for ec in range(NE):
                n = tau * 16 + ec
                bank = pbk[ec % 4]
                if n >= 4:
                    t.wait_ge(sm.s_sf, 8 + 3 * (n - 4) + 1)
                for u in range(4):
                    un = tau * NQ4 + ec * 4 + u
                    p = un % 2
                    t.wait_ge([sm.s_d0, sm.s_d1][p], 16 * (un // 2 + 1))
                    for gc in range(16):
                        gg = u * 16 + gc
                        inst = t.matmul(bank[:, :], wdb[p][:, gc, :],
                                        hsl[hperm[gg]][:, tau * 512:(tau + 1) * 512],
                                        start=(gg == 0), stop=(gg == NI - 1))
                    inst.then_inc(sm.s_dc, 1)
                if n >= 1:
                    unit_transposes(n - 1)
        unit_transposes(31)

    es.close()
    return nc



def _prep_shared(inp):
    f32 = np.float32
    wq = np.ascontiguousarray(
        np.asarray(inp["L0_attn_Wq"]).reshape(H, NE, 128, 128).transpose(0, 2, 1, 3), dtype=f32)
    wk = np.ascontiguousarray(
        np.asarray(inp["L0_attn_Wk"]).reshape(G, NE, 128, 128).transpose(0, 2, 1, 3), dtype=f32)
    wv = np.ascontiguousarray(
        np.asarray(inp["L0_attn_Wv"]).reshape(G, NE, 128, 128).transpose(0, 2, 1, 3), dtype=f32)
    bq = np.ascontiguousarray(np.asarray(inp["L0_attn_bq"]).T, dtype=f32)
    bk = np.ascontiguousarray(np.asarray(inp["L0_attn_bk"]).T, dtype=f32)
    bv = np.ascontiguousarray(np.asarray(inp["L0_attn_bv"]).T, dtype=f32)
    wup = np.ascontiguousarray(
        np.asarray(inp["L0_ffn_W_up"]).T.reshape(NE, 128, NI, 128).transpose(2, 1, 0, 3), dtype=f32)
    wg = np.ascontiguousarray(
        np.asarray(inp["L0_ffn_W_gate"]).T.reshape(NE, 128, NI, 128).transpose(2, 1, 0, 3), dtype=f32)
    wd = np.ascontiguousarray(
        np.asarray(inp["L0_ffn_W_down"]).T.reshape(NI, 128, NE, 128).transpose(2, 1, 0, 3), dtype=f32)
    suug = np.ascontiguousarray(np.asarray(inp["L0_ffn_SU_ug"]).reshape(NE, 128).T, dtype=f32)
    sudp = np.ascontiguousarray(np.asarray(inp["L0_ffn_SU_dp"]).reshape(NI, 128).T, dtype=f32)
    wsug = np.asarray(inp["L0_ffn_Wscale_ug"])
    wsup = np.ascontiguousarray(wsug[:INTER].reshape(NI, 128).T, dtype=f32)
    wsg = np.ascontiguousarray(wsug[INTER:].reshape(NI, 128).T, dtype=f32)
    wsdp = np.ascontiguousarray(np.asarray(inp["L0_ffn_Wscale_dp"]).reshape(NE, 128).T, dtype=f32)
    return dict(wq=wq, wk=wk, wv=wv, bq=bq, bk=bk, bv=bv, wup=wup, wgate=wg,
                wdown=wd, suug=suug, sudp=sudp, wsup=wsup, wsg=wsg, wsdp=wsdp)


def _quarters(parity):
    return [0, 3, 1, 2] if parity == 0 else [1, 2, 0, 3]


def _masks_for(parity):
    qmap = _quarters(parity)
    out = np.zeros((24, 128, 512), dtype=np.float32)
    j = 0
    for tau in (0, 1):
        oq = qmap[tau] * 512
        for sl in SCHED[tau]:
            c_abs = qmap[sl // 4] * 4 + sl % 4
            k_abs = c_abs * 128 + np.arange(128)[:, None]
            q_abs = oq + np.arange(512)[None, :]
            out[j] = (k_abs <= q_abs).astype(np.float32)
            j += 1
    return np.ascontiguousarray(out.transpose(1, 0, 2)).astype(BF)


_NC_CACHE = {}


def kernel(**inputs):
    shared = _prep_shared(inputs)
    x = np.asarray(inputs["L0_attn_x"], dtype=np.float32)
    masks = [_masks_for(0), _masks_for(1)]

    if "nc" not in _NC_CACHE:
        _NC_CACHE["nc"] = build_nc()
    nc = _NC_CACHE["nc"]

    in_maps = []
    for core in range(8):
        b = core // 2
        par = core % 2
        qmap = _quarters(par)
        x4 = np.ascontiguousarray(
            np.stack([x[b, q * 512:(q + 1) * 512, :] for q in qmap]))
        m = dict(shared)
        m["x4"] = x4
        m["masks"] = masks[par]
        in_maps.append(m)

    res = run_bass_kernel_spmd(nc, in_maps, core_ids=list(range(8)))

    out = np.empty((B, S, E), dtype=np.float32)
    for core in range(8):
        b = core // 2
        qmap = _quarters(core % 2)
        oarr = res.results[core]["out"]
        out[b, qmap[0] * 512:(qmap[0] + 1) * 512, :] = oarr[0:512]
        out[b, qmap[1] * 512:(qmap[1] + 1) * 512, :] = oarr[512:1024]
    return out


# version 13
# speedup vs baseline: 1.0534x; 1.0534x over previous
import numpy as np
import ml_dtypes
from contextlib import ExitStack

import concourse.bass as bass
import concourse.mybir as mybir
from concourse.bass_utils import run_bass_kernel_spmd

F32 = mybir.dt.float32
BF16 = mybir.dt.bfloat16
AF = mybir.ActivationFunctionType
AL = mybir.AluOpType
AX = mybir.AxisListType
BF = ml_dtypes.bfloat16

B, S, E, H, G, D = 4, 2048, 2048, 16, 4, 128
NE = 16
NI = 64
INTER = 8192
TOWN = 1024
SCHED = {0: [0, 1, 2, 3, 8, 9, 10, 11], 1: list(range(16))}
CSM = float(1.0 / np.sqrt(128.0))
KB = 1024


def _hadamard(n):
    h = np.array([[1.0]])
    while h.shape[0] < n:
        h = np.block([[h, h], [h, -h]])
    return h


def _bops_ordered(nchunk, by_ready):
    ops = []
    for st in range(nchunk.bit_length() - 1):
        step = 1 << st
        for c0 in range(nchunk):
            if not (c0 & step):
                blk = c0 >> (st + 1)
                ready = ((blk + 1) << (st + 1)) - 1
                ops.append((st, c0, c0 | step, ready))
    if by_ready:
        ops.sort(key=lambda t: (t[3], t[0], t[1]))
    return ops


def _slot_plan(nchunk, ops):
    slots = list(range(nchunk))
    free = [nchunk, nchunk + 1]
    plan = []
    for st, c0, c1, ready in ops:
        a, b_ = slots[c0], slots[c1]
        f0, f1 = free
        plan.append((a, b_, f0, f1, ready, st))
        slots[c0], slots[c1] = f0, f1
        free = [a, b_]
    return plan, slots


def build_nc():
    nc = bass.Bass("TRN2", target_bir_lowering=False, detect_race_conditions=False)

    dx = nc.dram_tensor("x4", [4, 512, E], F32, kind="ExternalInput").ap()
    dwq = nc.dram_tensor("wq", [H, 128, NE, 128], F32, kind="ExternalInput").ap()
    dwk = nc.dram_tensor("wk", [G, 128, NE, 128], F32, kind="ExternalInput").ap()
    dwv = nc.dram_tensor("wv", [G, 128, NE, 128], F32, kind="ExternalInput").ap()
    dbq = nc.dram_tensor("bq", [128, H], F32, kind="ExternalInput").ap()
    dbk = nc.dram_tensor("bk", [128, G], F32, kind="ExternalInput").ap()
    dbv = nc.dram_tensor("bv", [128, G], F32, kind="ExternalInput").ap()
    dwup = nc.dram_tensor("wup", [NI, 128, NE, 128], F32, kind="ExternalInput").ap()
    dwg = nc.dram_tensor("wgate", [NI, 128, NE, 128], F32, kind="ExternalInput").ap()
    dwd = nc.dram_tensor("wdown", [NE, 128, NI, 128], F32, kind="ExternalInput").ap()
    dsuug = nc.dram_tensor("suug", [128, NE], F32, kind="ExternalInput").ap()
    dsudp = nc.dram_tensor("sudp", [128, NI], F32, kind="ExternalInput").ap()
    dwsup = nc.dram_tensor("wsup", [128, NI], F32, kind="ExternalInput").ap()
    dwsg = nc.dram_tensor("wsg", [128, NI], F32, kind="ExternalInput").ap()
    dwsdp = nc.dram_tensor("wsdp", [128, NE], F32, kind="ExternalInput").ap()
    dmask = nc.dram_tensor("masks", [128, 24, 512], BF16, kind="ExternalInput").ap()
    dout = nc.dram_tensor("out", [TOWN, E], F32, kind="ExternalOutput").ap()
    dqsp = nc.dram_tensor("q_sp", [H, 2, 128, 512], BF16, kind="Internal").ap()
    dasp = nc.dram_tensor("a_sp", [H, 2, 128, 512], BF16, kind="Internal").ap()
    dlsp = nc.dram_tensor("l_sp", [32, 512], F32, kind="Internal").ap()

    h128np = _hadamard(128)
    ident_h = nc.inline_tensor(np.eye(128, dtype=BF), "identc")
    h2048_h = nc.inline_tensor((h128np / np.sqrt(2048.0)).astype(BF), "h2048c")
    h8192_h = nc.inline_tensor((h128np / np.sqrt(8192.0)).astype(BF), "h8192c")
    h8192n_h = nc.inline_tensor((-h128np / np.sqrt(8192.0)).astype(BF), "h8192nc")
    onec_h = nc.inline_tensor(np.ones((128, 1), dtype=BF), "onecc")
    epsc_h = nc.inline_tensor(np.full((128, 1), 1e-5, dtype=np.float32), "epscc")
    oner_h = nc.inline_tensor(np.ones((1, 128), dtype=np.float32), "onerc")
    identf_h = nc.inline_tensor(np.eye(128, dtype=np.float32), "identfc")

    SB_BASE = 17408
    def at(name, shape, dtype, off):
        return nc.alloc_sbuf_tensor_at(name, shape, dtype, offset=SB_BASE + off).ap()

    o = 0
    def nxt(name, shape, dtype, nbytes):
        nonlocal o
        ap_ = at(name, shape, dtype, o)
        o += nbytes
        return ap_

    ident = nxt("ident", [128, 128], BF16, 256)
    h2048 = nxt("h2048", [128, 128], BF16, 256)
    h8192 = nxt("h8192", [128, 128], BF16, 256)
    h8192n = nxt("h8192n", [128, 128], BF16, 256)
    onec = nxt("onec", [128, 1], BF16, 32)
    epsv = nxt("epsv", [128, 1], F32, 32)
    oner = nxt("oner", [1, 128], F32, 512)
    identf = nxt("identf", [128, 128], F32, 512)
    hdc = nxt("hdc", [128, NE, 128], BF16, NE * 256)
    bqs = nxt("bqs", [128, H], F32, 64)
    bks = nxt("bks", [128, G], F32, 32)
    bvs = nxt("bvs", [128, G], F32, 32)
    suugs = nxt("suugs", [128, NE], F32, 64)
    sudps = nxt("sudps", [128, NI], F32, 256)
    wsups = nxt("wsups", [128, NI], F32, 256)
    wsgs = nxt("wsgs", [128, NI], F32, 256)
    wsdps = nxt("wsdps", [128, NE], F32, 64)
    updps = nxt("updps", [128, NI], F32, 256)
    negb = nxt("negb", [128, H], F32, 64)
    qrow = nxt("qrow", [1, H, 2], F32, 128)
    krow = nxt("krow", [1, G, 4], F32, 64)
    mqr = nxt("mqr", [1, H], F32, 64)
    mkr = nxt("mkr", [1, G], F32, 32)
    mqk = nxt("mqk", [1, H], F32, 64)
    brow = nxt("brow", [1, H], F32, 64)
    nbrow = nxt("nbrow", [1, H], F32, 64)
    stats = nxt("stats", [128, 24], F32, 96)
    msq = nxt("msq", [128, 8], F32, 32)
    lnt = nxt("lnt", [128, 16], F32, 64)
    rrow = at("rrow", [1, 512], F32, o)
    hscr0_off = o
    o += 2048
    R0_END = o
    assert R0_END <= 10 * KB + 512, R0_END

    R1 = R0_END
    R2 = R1 + 24 * KB
    R3 = R2 + 16 * KB
    R4 = R3 + 8 * KB
    R5 = R4 + 16 * KB
    R6 = R5 + 32 * KB
    R7 = R6 + 32 * KB
    R8 = R7 + 32 * KB
    R9 = R8 + 16 * KB
    R10 = R9 + 16 * KB
    SBUF_END = 229376 - 17408

    masksb = at("masksb", [128, 24, 512], BF16, R1)
    xrowf = at("xrowf", [128, E], F32, R2)
    wub = [at("wub0", [128, NE, 128], BF16, R2), at("wub1", [128, NE, 128], BF16, R2 + 4 * KB)]
    wgb = [at("wgb0", [128, NE, 128], BF16, R2 + 8 * KB), at("wgb1", [128, NE, 128], BF16, R2 + 12 * KB)]
    wdb = [at("wdb0", [128, NE, 128], BF16, R2 + 8 * KB), at("wdb1", [128, NE, 128], BF16, R2 + 12 * KB)]
    xntok = [at("xntok0", [128, E], BF16, R3), at("xntok1", [128, E], BF16, R3 + 4 * KB)]
    xnf = at("xnf", [128, NE, 512], BF16, R4)
    xnown = at("xnown", [128, NE, TOWN], BF16, R5)
    obst = at("obst", [128, 4, NE, 128], BF16, R5)
    wqkvb = [at("wqkv0", [128, NE, 128], BF16, R6), at("wqkv1", [128, NE, 128], BF16, R6 + 4 * KB)]
    zsl = [at("z%d" % i, [128, TOWN], BF16, R6 + 2 * KB * i) for i in range(16)]
    obstA = at("obstA", [128, 4, NE, 128], BF16, R6)
    outs = [at("outs0", [128, E], F32, R6 + 16 * KB)] * 2
    attn_s = [at("attns0", [128, 512], BF16, R6 + 24 * KB),
              at("attns1", [128, 512], BF16, R6 + 25 * KB)]
    lread = at("lread", [32, 512], F32, R6 + 27 * KB)
    lrr = at("lrr", [32, 512], F32, R6 + 29 * KB)
    rcolB = at("rcolB", [128, 8, 32], F32, R6 + 31 * KB)
    lrow_s = [at("lrow0", [1, 512], F32, R7 + 12 * KB),
              at("lrow1", [1, 512], F32, R7 + 14 * KB)]
    zscr = at("zscr", [128, TOWN], BF16, R7)
    qstage = [at("qst0", [128, 512], BF16, R7), at("qst1", [128, 512], BF16, R7 + KB)]
    xsq = at("xsq", [128, 2048], F32, R2)
    xst = [at("xst0", [128, E], F32, R6 + 8 * KB),
           at("xst1", [128, E], F32, R6 + 16 * KB),
           at("xst2", [128, E], F32, R6 + 24 * KB)]
    vstage = at("vst", [128, 512], BF16, R7 + 2 * KB)
    qsq = at("qsq", [128, 512], BF16, R7 + 3 * KB)
    sfst = at("sfst", [128, 512], BF16, R6 + 26 * KB)
    pbuf = [at("pb_0", [128, 512], BF16, R7 + 4 * KB), at("pb_1", [128, 512], BF16, R7 + 5 * KB),
            at("pb_2", [128, 512], BF16, R7 + 2 * KB)]
    pmb = [at("pm_0", [128, 512], BF16, R7 + 6 * KB), at("pm_1", [128, 512], BF16, R7 + 7 * KB),
           at("pm_2", [128, 512], BF16, R7 + 3 * KB)]
    qd = [at("qd0", [128, 512], BF16, R7 + 8 * KB), at("qd1", [128, 512], BF16, R7 + 9 * KB)]
    asb = [at("asb0", [128, 512], BF16, R7 + 10 * KB), at("asb1", [128, 512], BF16, R7 + 11 * KB)]
    silu_s = at("sil0", [128, TOWN], BF16, R7 + 12 * KB)
    rrep = at("rrep", [128, 512], F32, R7 + 12 * KB)
    Kb = at("Kb", [128, G, S], BF16, R8)
    Vb = at("Vb", [128, G, 16, 128], BF16, R9)

    h_off = []
    h_off += [R1 + 2 * KB * i for i in range(12)]
    h_off += [R3 + 2 * KB * i for i in range(4)]
    h_off += [R4 + 2 * KB * i for i in range(8)]
    h_off += [R7 + 0 * KB, R7 + 2 * KB, R7 + 4 * KB, R7 + 6 * KB, R7 + 8 * KB, R7 + 10 * KB]
    h_off += [R7 + 16 * KB + 2 * KB * i for i in range(8)]
    h_off += [R8 + 2 * KB * i for i in range(8)]
    h_off += [R9 + 2 * KB * i for i in range(8)]
    h_off += [R5 + 16 * KB + 2 * KB * i for i in range(8)]
    n_tail = (SBUF_END - R10) // (2 * KB)
    h_off += [R10 + 2 * KB * i for i in range(n_tail)]
    assert len(h_off) >= 64, len(h_off)
    hsl = [at("h%d" % i, [128, TOWN], BF16, h_off[i]) for i in range(64)]
    hsl.append(at("h64", [128, TOWN], BF16, hscr0_off))
    hsl.append(at("h65", [128, TOWN], BF16, R7 + 14 * KB))

    pbk = [nc.alloc_psum_tensor("pbk%d" % i, [128, 512], F32).ap() for i in range(8)]

    es = ExitStack()
    SEMS = [
        "s_cst", "s_x0", "s_x1", "s_x2", "s_x3", "s_xf", "s_xd", "s_ln", "s_sq", "s_lnt", "s_lne", "s_xn",
        "s_w0", "s_w1", "s_wc", "s_ev", "s_sn", "s_snm",
        "s_qs0", "s_qs1", "s_vt", "s_vev",
        "s_b2", "s_pe", "s_b3", "s_b4", "s_prep",
        "s_zm", "s_zb",
        "s_qr0", "s_qr1", "s_sc", "s_px", "s_pm", "s_av", "s_rr", "s_rp",
        "s_ao", "s_aw0", "s_aw1",
        "s_u0", "s_u1", "s_g0", "s_g1", "s_uc", "s_gc", "s_sl", "s_h",
        "s_m5", "s_m5e", "s_h4",
        "s_d0", "s_d1", "s_dc", "s_sf", "s_ft",
        "s_ar0", "s_ar1",
        "s_oo", "s_ow", "s_vg", "s_lw", "s_lr",
    ]
    sems = {n: es.enter_context(nc.semaphore(n)) for n in SEMS}
    sm = type("S", (), sems)
    block = es.enter_context(nc.Block())

    n_cst = 17
    def _kv(ts):
        return [("k", g, ts) for g in range(G)] + [("v", g, ts) for g in range(G)]
    wstream = (_kv(0) + [("q", h, 0) for h in range(12)]
               + _kv(1) + [("q", h, 0) for h in range(12, 16)] + [("q", h, 1) for h in range(8)]
               + _kv(2) + [("q", h, 1) for h in range(8, 16)]
               + _kv(3))
    NW = len(wstream)
    _release = [4 * ts + 4 for (_k, _i, ts) in wstream]
    POPS = 5
    _wpa = {}
    _wp = 0
    for _bi in range(16):
        _ne = 0
        while _ne < NW and _release[_ne] <= _bi:
            _ne += 1
        _wp = min(_wp + POPS, _ne)
        _wpa[_bi] = _wp
    sq_point = {bi: (_wpa[bi] if (bi + 1) % 4 == 0 else _wpa[min(bi + 1, 15)]) - 1
                for bi in range(16)}
    stat_drains = [n for n in range(NW)
                   if n > 0 and wstream[n][0] == "k" and wstream[n][2] >= 1
                   and wstream[n - 1][2] != wstream[n][2]]
    _kqpos = [n for n in range(NW) if wstream[n][0] != "v"]
    lnt_pre = {n: [] for n in range(NW)}
    lnt_post = {n: [] for n in range(NW)}
    for bi in range(16):
        W = _wpa[bi] if (bi + 1) % 4 == 0 else _wpa[min(bi + 1, 15)]
        lastkq = max((q for q in _kqpos if q < W), default=-1)
        if lastkq < 0:
            lnt_pre.setdefault(-1, []).append(bi)
            continue
        nxt = next((q for q in _kqpos if q > lastkq), NW)
        d = next((d_ for d_ in stat_drains if lastkq < d_ <= nxt), None)
        if d is not None:
            lnt_pre[d].append(bi)
        else:
            lnt_post[nxt].append(bi)
    def _Wf(j):
        return _wpa[j] if (j + 1) % 4 == 0 else _wpa[min(j + 1, 15)]
    xq_need = {}
    for bi in range(16):
        j = bi - 3
        W = _Wf(j) if j >= 0 else 0
        ms = [wstream[p][1] + 16 * wstream[p][2] for p in range(W) if wstream[p][0] == "q"]
        xq_need[bi] = (max(ms) - 2) if ms else -1
    xspill_trig = {}
    for bi in range(16):
        if xq_need[bi] >= 0:
            xspill_trig.setdefault(xq_need[bi], []).append(bi)
    pos_q = {}
    vi_of = {}
    stat_of = {}
    nvv = 0
    nst = 0
    for n, (kind, idx, ts) in enumerate(wstream):
        if kind == "v":
            vi_of[n] = nvv
            nvv += 1
        else:
            stat_of[n] = nst
            nst += 1
            if kind == "q":
                pos_q[ts * 16 + idx] = n
    NSTAT = nst

    jobs3 = []
    for tau in (0, 1):
        for h in range(H):
            for js, sl in enumerate(SCHED[tau]):
                jobs3.append((tau, h, js, sl, tau * 16 + h))
    NJ3 = len(jobs3)
    grp_last = {}
    for j, (tau, h, js, sl, n) in enumerate(jobs3):
        grp_last[n] = j

    hops = [op for op in _bops_ordered(64, by_ready=True) if op[0] < 5]
    hplan, hfin = _slot_plan(64, hops)
    hperm = [hfin[c] for c in range(64)]
    ord4 = {}
    _c4 = 0
    for _st, _c0, _c1, _r in hops:
        if _st == 4:
            _c4 += 1
            ord4[_c0] = ord4[_c1] = _c4
    p5pairs = sorted(range(32), key=lambda c0: (max(ord4[c0], ord4[c0 + 32]), c0))
    NQ4 = NE * 4

    @block.sync
    def _(sync):
        sync.dma_start(out=epsv[:, :], in_=epsc_h.ap()[:, :]).then_inc(sm.s_cst, 16)
        sync.dma_start(out=ident[:, :], in_=ident_h.ap()[:, :]).then_inc(sm.s_cst, 16)
        sync.dma_start(out=h2048[:, :], in_=h2048_h.ap()[:, :]).then_inc(sm.s_cst, 16)
        sync.dma_start(out=h8192[:, :], in_=h8192_h.ap()[:, :]).then_inc(sm.s_cst, 16)
        sync.dma_start(out=h8192n[:, :], in_=h8192n_h.ap()[:, :]).then_inc(sm.s_cst, 16)
        sync.dma_start(out=onec[:, :], in_=onec_h.ap()[:, :]).then_inc(sm.s_cst, 16)
        sync.dma_start(out=oner[:, :], in_=oner_h.ap()[:, :]).then_inc(sm.s_cst, 16)
        sync.dma_start(out=identf[:, :], in_=identf_h.ap()[:, :]).then_inc(sm.s_cst, 16)
        sync.dma_start(out=masksb[:, :, :], in_=dmask[:, :, :]).then_inc(sm.s_cst, 16)
        for dst, src in [(bqs, dbq), (bks, dbk), (bvs, dbv), (suugs, dsuug),
                         (sudps, dsudp), (wsups, dwsup), (wsgs, dwsg), (wsdps, dwsdp)]:
            sync.dma_start(out=dst[:, :], in_=src[:, :]).then_inc(sm.s_cst, 16)
        def xload(bi):
            p = bi % 3
            if bi >= 3:
                sync.wait_ge(sm.s_xd, bi - 2)
            sync.dma_start(
                out=xst[p][:, :], in_=dx[bi // 4, (bi % 4) * 128:(bi % 4) * 128 + 128, :]
            ).then_inc([sm.s_x0, sm.s_x1, sm.s_x2][p], 16)

        for bi in range(16):
            if xq_need[bi] < 0:
                xload(bi)
        for m in range(32):
            p = m % 2
            sync.wait_ge(sm.s_ev, pos_q[m] + 1)
            sync.dma_start(out=dqsp[m % 16, m // 16, :, :], in_=qstage[p][:, :]).then_inc(
                [sm.s_qs0, sm.s_qs1][p], 16)
            for bi in xspill_trig.get(m, []):
                xload(bi)
        for n in range(34):
            if n < 32:
                p = n % 2
                sync.wait_ge([sm.s_qs0, sm.s_qs1][p], 16 * (n // 2 + 1))
                if n >= 2:
                    sync.wait_ge(sm.s_ao, n - 1)
                sync.dma_start(out=qd[p][:, :], in_=dqsp[n % 16, n // 16, :, :]).then_inc(
                    [sm.s_qr0, sm.s_qr1][p], 16)
            if n >= 2:
                m = n - 2
                p = m % 2
                sync.wait_ge(sm.s_ao, m + 1)
                sync.dma_start(out=dasp[m % 16, m // 16, :, :], in_=asb[p][:, :]).then_inc(
                    [sm.s_aw0, sm.s_aw1][p], 16)
                sync.dma_start(out=dlsp[m:m + 1, :], in_=lrow_s[p][0:1, :]).then_inc(
                    sm.s_lw, 16)
        sync.wait_ge(sm.s_uc, NI)
        sync.wait_ge(sm.s_gc, NI)
        sync.dma_start(out=lread[:, :], in_=dlsp[:, :]).then_inc(sm.s_lr, 16)
        for tau in range(2):
            for sb in range(4):
                qb = tau * 4 + sb
                sync.wait_ge(sm.s_xd, 16 + qb)
                sync.dma_start(
                    out=xrowf[:, :], in_=dx[tau, sb * 128:(sb + 1) * 128, :]
                ).then_inc(sm.s_xf, 16)
                sync.wait_ge(sm.s_oo, qb + 1)
                sync.dma_start(out=dout[qb * 128:(qb + 1) * 128, :], in_=outs[qb % 2][:, :]).then_inc(
                    sm.s_ow, 16)
        sync.wait_ge(sm.s_ow, 16 * 8)

    @block.gpsimd
    def _(g):
        for n, (kind, idx, ts) in enumerate(wstream):
            p = n % 2
            if n >= 2:
                g.wait_ge(sm.s_wc, n - 1)
            src = {"k": dwk, "v": dwv, "q": dwq}[kind]
            g.dma_start(out=wqkvb[p][:, :, :], in_=src[idx, :, :, :]).then_inc(
                [sm.s_w0, sm.s_w1][p], 16)
        g.wait_ge(sm.s_ln, 16)
        for ic in range(NI):
            p = ic % 2
            if ic >= 2:
                g.wait_ge(sm.s_uc, ic - 1)
            g.dma_start(out=wub[p][:, :, :], in_=dwup[ic, :, :, :]).then_inc(
                [sm.s_u0, sm.s_u1][p], 16)
            if ic >= 2:
                g.wait_ge(sm.s_gc, ic - 1)
            g.dma_start(out=wgb[p][:, :, :], in_=dwg[ic, :, :, :]).then_inc(
                [sm.s_g0, sm.s_g1][p], 16)
        g.wait_ge(sm.s_gc, NI)
        for tau in range(2):
            for u in range(NQ4):
                n = tau * NQ4 + u
                ec, q4 = u // 4, u % 4
                p = n % 2
                if n >= 2:
                    g.wait_ge(sm.s_dc, n - 1)
                g.dma_start(
                    out=wdb[p][:, :, :], in_=dwd[ec, :, q4 * 16:(q4 + 1) * 16, :]
                ).then_inc([sm.s_d0, sm.s_d1][p], 16)

    @block.vector
    def _(v):
        _vgc = [0]

        def vg(inst):
            _vgc[0] += 1
            inst.then_inc(sm.s_vg, 1)
            v.wait_ge(sm.s_vg, _vgc[0])

        _si = [0]

        def emit_norm(v, n, kind, idx, ts):
            if kind == "v":
                gi = vi_of[n]
                v.wait_ge(sm.s_vt, gi + 1)
                v.tensor_copy(
                    Vb[:, idx, ts * 4:ts * 4 + 4, :],
                    pbk[4 + gi % 2].bitcast(BF16)[:, 0:512].rearrange("p (s d) -> p s d", s=4),
                ).then_inc(sm.s_vev, 1)
                return
            si = _si[0]
            v.wait_ge(sm.s_ev, n + 1)
            if si > 0:
                v.wait_ge(sm.s_snm, si)
            if kind == "k":
                kt = Kb[:, idx, ts * 512:(ts + 1) * 512]
                v.tensor_mul(qsq[:, :], kt, kt).then_inc(sm.s_sn, 1)
                v.wait_ge(sm.s_snm, si + 1)
                v.reduce_max(krow[0:1, idx, ts:ts + 1], pbk[6][0:1, :], axis=AX.X).then_inc(sm.s_sn, 1)
            else:
                m = ts * 16 + idx
                qt = qstage[m % 2]
                v.tensor_mul(qsq[:, :], qt[:, :], qt[:, :]).then_inc(sm.s_sn, 1)
                v.wait_ge(sm.s_snm, si + 1)
                v.reduce_max(qrow[0:1, idx, ts:ts + 1], pbk[6][0:1, :], axis=AX.X).then_inc(sm.s_sn, 1)
            _si[0] = si + 1
        wpos = 0

        def lc(bi, k):
            b2 = 8 * (bi % 2)
            return lnt[:, b2 + k:b2 + k + 1]

        def ln_stats(bi):
            p = bi % 3
            v.wait_ge([sm.s_x0, sm.s_x1, sm.s_x2][p], 16 * (bi // 3 + 1))
            v.tensor_mul(xsq[:, :], xst[p][:, :], xst[p][:, :])
            v.reduce_sum(lc(bi, 0), xst[p][:, :], axis=AX.X)
            vg(v.reduce_sum(lc(bi, 1), xsq[:, :], axis=AX.X))
            v.tensor_scalar_mul(lc(bi, 3), lc(bi, 0), 1.0 / 2048.0)
            vg(v.tensor_scalar_mul(lc(bi, 4), lc(bi, 1), 1.0 / 2048.0))
            vg(v.tensor_mul(lc(bi, 5), lc(bi, 3), lc(bi, 3)))
            v.tensor_sub(lc(bi, 4), lc(bi, 4), lc(bi, 5)).then_inc(sm.s_ln, 1)

        def ln_finish(bi):
            p = bi % 3
            ts = bi // 4
            v.wait_ge(sm.s_sq, bi + 1)
            vg(v.reciprocal(lc(bi, 6), lc(bi, 7)))
            vg(v.tensor_mul(lc(bi, 5), lc(bi, 3), lc(bi, 6)))
            v.tensor_scalar(
                xntok[bi % 2][:, :], xst[p][:, :], lc(bi, 6), lc(bi, 5),
                AL.mult, AL.subtract,
            ).then_inc(sm.s_xd, 1)
            v.wait_ge(sm.s_lnt, 2 * bi + 2)
            if bi == 12:
                v.wait_ge(sm.s_wc, 48)
            for half in range(2):
                srcv = pbk[half].bitcast(BF16)[:, 0:1024]
                if ts < 2:
                    dstv = xnown[:, 8 * half:8 * half + 8,
                                 ts * 512 + (bi % 4) * 128: ts * 512 + (bi % 4) * 128 + 128]
                else:
                    dstv = xnf[:, 8 * half:8 * half + 8, (bi % 4) * 128:(bi % 4) * 128 + 128]
                v.tensor_copy(dstv, srcv.rearrange("p (c t) -> p c t", c=8)).then_inc(sm.s_lne, 1)
            v.engine_nop().then_inc(sm.s_xn, 1)

        norm_end = 0

        def pop_norms(k):
            nonlocal wpos
            while k > 0 and wpos < norm_end:
                n = wpos
                kind, idx, _ts = wstream[n]
                emit_norm(v, n, kind, idx, _ts)
                wpos += 1
                k -= 1

        for bi in range(16):
            if bi % 4 == 0 and bi >= 1:
                ln_finish(bi - 1)
                if bi == 4:
                    v.wait_ge(sm.s_cst, 16 * n_cst)
                    for c in range(NE):
                        v.tensor_scalar_mul(hdc[:, c, :], h2048[:, :], suugs[:, c:c + 1])
                    v.tensor_mul(updps[:, :], wsups[:, :], sudps[:, :]).then_inc(sm.s_prep, 1)
            norm_end = 0
            while norm_end < NW and _release[norm_end] <= bi:
                norm_end += 1
            pop_norms(POPS)
            ln_stats(bi)
            if bi % 4 != 0 and bi >= 1:
                ln_finish(bi - 1)
        ln_finish(15)
        norm_end = NW
        pop_norms(NW)
        v.engine_nop()
        vg(v.engine_nop())
        v.reduce_max(mqr[0:1, :], qrow[0:1, :, :], axis=AX.X)
        vg(v.reduce_max(mkr[0:1, :], krow[0:1, :, :], axis=AX.X))
        v.tensor_mul(
            mqk[0:1, :].rearrange("o (g r) -> o g r", g=G),
            mqr[0:1, :].rearrange("o (g r) -> o g r", g=G),
            mkr[0:1, :].unsqueeze(2).broadcast_to([1, G, 4]),
        ).then_inc(sm.s_b2, 1)
        v.wait_ge(sm.s_pe, 1)
        v.tensor_scalar_mul(nbrow[0:1, :], brow[0:1, :], -1.0).then_inc(sm.s_b3, 1)
        v.wait_ge(sm.s_b4, 1)
        v.tensor_copy(negb[:, :], pbk[6][:, 0:H]).then_inc(sm.s_b4, 1)
        for c in range(NE):
            v.wait_ge(sm.s_zm, c + 1)
            bk_ = c % 4
            v.tensor_copy(zsl[c][:, 0:512], pbk[bk_][:, :])
            v.tensor_copy(zsl[c][:, 512:1024], pbk[4 + bk_][:, :]).then_inc(sm.s_zb, 1)
        zops = [(c0, c0 | (1 << st)) for st in range(4) for c0 in range(16) if not (c0 & (1 << st))]
        fin_pend = []

        def fin_emit(n, jl):
            v.wait_ge(sm.s_av, jl + 1)
            if n >= 2:
                v.wait_ge([sm.s_aw0, sm.s_aw1][n % 2], 16 * ((n - 2) // 2 + 1))
                v.wait_ge(sm.s_lw, 16 * (n - 1))
            v.tensor_copy(lrow_s[n % 2][0:1, :], pbk[4 + n % 2][0:1, :])
            v.tensor_copy(asb[n % 2][:, :], pbk[2 + n % 2][:, :]).then_inc(sm.s_ao, 1)
            if zops and n >= 2:
                c0, c1 = zops.pop(0)
                v.tensor_sub(zscr[:, :], zsl[c0][:, :], zsl[c1][:, :])
                v.tensor_add(zsl[c0][:, :], zsl[c0][:, :], zsl[c1][:, :])
                v.tensor_copy(zsl[c1][:, :], zscr[:, :])

        for j, (tau, h, js, sl, n) in enumerate(jobs3):
            p = j % 3
            mi = (8 + js) if tau else js
            v.wait_ge(sm.s_px, j + 1)
            if j >= 3:
                v.wait_ge(sm.s_av, j - 2)
            v.tensor_mul(pmb[p][:, :], pbuf[p][:, :], masksb[:, mi, :]).then_inc(sm.s_pm, 1)
            if j == grp_last[n]:
                fin_pend.append((j + 3, n, j))
            while fin_pend and fin_pend[0][0] <= j:
                _, n_, jl_ = fin_pend.pop(0)
                fin_emit(n_, jl_)
        for _, n_, jl_ in fin_pend:
            fin_emit(n_, jl_)
        for c0, c1 in zops:
            v.tensor_sub(zscr[:, :], zsl[c0][:, :], zsl[c1][:, :])
            v.tensor_add(zsl[c0][:, :], zsl[c0][:, :], zsl[c1][:, :])
            v.tensor_copy(zsl[c1][:, :], zscr[:, :])
        v.engine_nop().then_inc(sm.s_prep, 1)
        v.wait_ge(sm.s_aw0, 16 * 16)
        v.wait_ge(sm.s_aw1, 16 * 16)
        v.wait_ge(sm.s_qs0, 16 * 16)
        v.wait_ge(sm.s_qs1, 16 * 16)
        v.wait_ge(sm.s_lw, 16 * 32)
        hq = list(hplan)
        for ic in range(NI):
            bu = [pbk[0], pbk[1]] if ic % 2 == 0 else [pbk[4], pbk[5]]
            v.wait_ge(sm.s_uc, ic + 1)
            v.wait_ge(sm.s_sl, 2 * (ic + 1))
            for tau in range(2):
                v.scalar_tensor_tensor(
                    hsl[ic][:, tau * 512:(tau + 1) * 512],
                    bu[tau][:, :], updps[:, ic:ic + 1],
                    silu_s[:, tau * 512:(tau + 1) * 512],
                    AL.mult, AL.mult,
                )
            v.engine_nop().then_inc(sm.s_h, 1)
            while hq and hq[0][4] <= ic:
                a, b_, f0, f1, _r, st_ = hq.pop(0)
                v.tensor_add(hsl[f0][:, :], hsl[a][:, :], hsl[b_][:, :])
                inst = v.tensor_sub(hsl[f1][:, :], hsl[a][:, :], hsl[b_][:, :])
                if st_ == 4:
                    inst.then_inc(sm.s_h4, 1)
        assert not hq
        for k, c0 in enumerate(p5pairs):
            pp = k % 2
            v.wait_ge(sm.s_m5, k + 1)
            v.tensor_copy(hsl[hperm[c0]][:, 0:512], pbk[4 * pp][:, :])
            v.tensor_copy(hsl[hperm[c0]][:, 512:1024], pbk[4 * pp + 1][:, :]).then_inc(sm.s_m5e, 1)
            v.tensor_copy(hsl[hperm[c0 + 32]][:, 0:512], pbk[4 * pp + 2][:, :])
            v.tensor_copy(hsl[hperm[c0 + 32]][:, 512:1024], pbk[4 * pp + 3][:, :]).then_inc(sm.s_m5e, 1)
        v.wait_ge(sm.s_lr, 16)
        v.reciprocal(lrr[:, :], lread[:, :]).then_inc(sm.s_lw, 1)
        for qb in range(8):
            v.wait_ge(sm.s_ft, qb + 1)
            v.tensor_copy(rcolB[:, qb, :], pbk[6 + qb % 2][:, 0:32]).then_inc(sm.s_sf, 1)
        for tau in range(2):
            for ec in range(NE):
                n = tau * 16 + ec
                v.wait_ge(sm.s_dc, tau * NQ4 + (ec + 1) * 4)
                if n >= 1:
                    v.wait_ge(sm.s_ft, 8 + 2 * n)
                v.tensor_scalar_mul(sfst[:, :], pbk[ec % 4][:, :], wsdps[:, ec:ec + 1]).then_inc(sm.s_sf, 1)
                v.wait_ge(sm.s_ft, 8 + 2 * n + 1)
                v.tensor_copy(
                    obst[:, :, ec, :],
                    pbk[4 + n % 2].bitcast(BF16)[:, 0:512].rearrange("p (s d) -> p s d", s=4),
                ).then_inc(sm.s_sf, 1)
                v.wait_ge(sm.s_ft, 8 + 2 * n + 2)
                v.tensor_copy(
                    obstA[:, :, ec, :],
                    pbk[6 + n % 2].bitcast(BF16)[:, 0:512].rearrange("p (s d) -> p s d", s=4),
                ).then_inc(sm.s_sf, 1)
            vg(v.engine_nop())
            for sb in range(4):
                qb = tau * 4 + sb
                v.wait_ge(sm.s_xf, 16 * (qb + 1))
                if qb >= 1:
                    v.wait_ge(sm.s_ow, 16 * qb)
                for h_ in range(H):
                    inst = v.scalar_tensor_tensor(
                        outs[qb % 2][:, h_ * 128:(h_ + 1) * 128],
                        obstA[:, sb, h_, :],
                        rcolB[:, qb, tau * 16 + h_:tau * 16 + h_ + 1],
                        obst[:, sb, h_, :],
                        AL.mult, AL.add,
                    )
                vg(inst)
                v.tensor_add(outs[qb % 2][:, :], outs[qb % 2][:, :], xrowf[:, :]).then_inc(sm.s_oo, 1)
                v.engine_nop().then_inc(sm.s_xd, 1)

    @block.scalar
    def _(s):
        s.wait_ge(sm.s_cst, 16)
        nv = 0
        swpos = 0
        first_ev = [True]
        def sqrt_of(bi):
            b2 = 8 * (bi % 2)
            s.wait_ge(sm.s_ln, bi + 1)
            s.activation(lnt[:, b2 + 7:b2 + 8], lnt[:, b2 + 4:b2 + 5], AF.Sqrt,
                         bias=epsv[:, 0:1], scale=1.0).then_inc(sm.s_sq, 1)

        from collections import defaultdict
        sq_trig = defaultdict(list)
        for bi in range(16):
            if sq_point[bi] < 0:
                sqrt_of(bi)
            else:
                sq_trig[sq_point[bi]].append(bi)
        for n, (kind, idx, ts) in enumerate(wstream):
            if True:
                bank = [pbk[2], pbk[3], pbk[7]][n % 3]
                if first_ev[0]:
                    s.wait_ge(sm.s_cst, 16 * n_cst)
                    first_ev[0] = False
                s.wait_ge(sm.s_wc, n + 1)
                if kind == "k":
                    s.activation(Kb[:, idx, ts * 512:(ts + 1) * 512], bank[:, :], AF.Identity,
                                 bias=bks[:, idx:idx + 1]).then_inc(sm.s_ev, 1)
                elif kind == "v":
                    nv += 1
                    if nv > 1:
                        s.wait_ge(sm.s_vt, nv - 1)
                    s.activation(vstage[:, :], bank[:, :], AF.Identity,
                                 bias=bvs[:, idx:idx + 1]).then_inc(sm.s_ev, 1)
                else:
                    m = ts * 16 + idx
                    p = m % 2
                    if m >= 2:
                        s.wait_ge([sm.s_qs0, sm.s_qs1][p], 16 * ((m - 2) // 2 + 1))
                        s.wait_ge(sm.s_sn, 2 * stat_of[pos_q[m - 2]] + 1)
                    s.activation(qstage[p][:, :], bank[:, :], AF.Identity,
                                 bias=bqs[:, idx:idx + 1]).then_inc(sm.s_ev, 1)
                for bi in sq_trig[n]:
                    sqrt_of(bi)
        s.wait_ge(sm.s_b2, 1)
        s.activation(brow[0:1, :], mqk[0:1, :], AF.Sqrt, scale=1.0 / 128.0).then_inc(sm.s_pe, 1)
        s.wait_ge(sm.s_b4, 2)
        sb3s = [pbk[0], pbk[1], pbk[6], pbk[7]]
        for j, (tau, h, js, sl, n) in enumerate(jobs3):
            p = j % 3
            s.wait_ge(sm.s_sc, j + 1)
            if j >= 3:
                s.wait_ge(sm.s_pm, j - 2)
            s.activation(pbuf[p][:, :], sb3s[j % 4][:, :], AF.Exp,
                         bias=negb[:, h:h + 1], scale=CSM).then_inc(sm.s_px, 1)
        s.wait_ge(sm.s_lw, 16 * 32)
        for ic in range(NI):
            bg = [pbk[2], pbk[3]] if ic % 2 == 0 else [pbk[6], pbk[7]]
            s.wait_ge(sm.s_gc, ic + 1)
            if ic >= 1:
                s.wait_ge(sm.s_h, ic)
            for tau in range(2):
                s.activation(silu_s[:, tau * 512:(tau + 1) * 512], bg[tau][:, :],
                             AF.Silu, scale=wsgs[:, ic:ic + 1]).then_inc(sm.s_sl, 1)
        for tau in range(2):
            for ec in range(NE):
                n = tau * 16 + ec
                p = n % 2
                s.wait_ge([sm.s_aw0, sm.s_aw1][p], 16 * (n // 2 + 1))
                if n >= 2:
                    s.wait_ge(sm.s_ft, 8 + 2 * (n - 2) + 2)
                s.dma_start(out=attn_s[p][:, :], in_=dasp[ec, tau, :, :]).then_inc(
                    [sm.s_ar0, sm.s_ar1][p], 16)

    @block.tensor
    def _(t):
        t.wait_ge(sm.s_cst, 32)
        first_proj = [True]
        from collections import deque
        statq = deque()
        nv = 0
        wpos = 0
        def lnt_of(bi):
            p = bi % 2
            t.wait_ge(sm.s_xd, bi + 1)
            if bi >= 1:
                t.wait_ge(sm.s_lne, 2 * bi)
            for half in range(2):
                dstb = pbk[half].bitcast(BF16)
                for i in range(8):
                    ec = half * 8 + i
                    inst = t.transpose(dstb[:, i * 128:(i + 1) * 128],
                                       xntok[p][:, ec * 128:(ec + 1) * 128], ident[:, :])
                inst.then_inc(sm.s_lnt, 1)

        def stat_drain():
            while statq:
                sj = statq.popleft()
                t.wait_ge(sm.s_sn, 2 * sj + 1)
                t.matmul(pbk[6][0:1, :], onec[:, :], qsq[:, :], start=True,
                         stop=True).then_inc(sm.s_snm, 1)

        for bi in lnt_pre.get(-1, []):
            lnt_of(bi)
        for n, (kind, idx, ts) in enumerate(wstream):
            if True:
                if n in stat_drains:
                    stat_drain()
                for bi in lnt_pre[n]:
                    lnt_of(bi)
                p = n % 2
                bank = [pbk[2], pbk[3], pbk[7]][n % 3]
                if first_proj[0]:
                    t.wait_ge(sm.s_cst, 16 * n_cst)
                    first_proj[0] = False
                t.wait_ge([sm.s_w0, sm.s_w1][p], 16 * (n // 2 + 1))
                t.wait_ge(sm.s_xn, 4 * (ts + 1))
                if n >= 3:
                    t.wait_ge(sm.s_ev, n - 2)
                src = xnown[:, :, ts * 512:(ts + 1) * 512] if ts < 2 else xnf[:, :, :]
                for ec in range(NE):
                    inst = t.matmul(bank[:, :], wqkvb[p][:, ec, :], src[:, ec, :],
                                    start=(ec == 0), stop=(ec == NE - 1))
                inst.then_inc(sm.s_wc, 1)
                if statq:
                    sj = statq.popleft()
                    t.wait_ge(sm.s_sn, 2 * sj + 1)
                    t.matmul(pbk[6][0:1, :], onec[:, :], qsq[:, :], start=True, stop=True).then_inc(sm.s_snm, 1)
                if kind != "v":
                    statq.append(stat_of[n])
                else:
                    nv += 1
                    t.wait_ge(sm.s_ev, n + 1)
                    if nv >= 3:
                        t.wait_ge(sm.s_vev, nv - 2)
                    dstb = pbk[4 + (nv - 1) % 2].bitcast(BF16)
                    for i in range(4):
                        inst = t.transpose(dstb[:, i * 128:(i + 1) * 128],
                                           vstage[:, i * 128:(i + 1) * 128], ident[:, :])
                    inst.then_inc(sm.s_vt, 1)
                for bi in lnt_post[n]:
                    lnt_of(bi)
        stat_drain()
        t.wait_ge(sm.s_b3, 1)
        t.matmul(pbk[6][:, 0:H], oner[0:1, :], nbrow[0:1, :], start=True, stop=True).then_inc(sm.s_b4, 1)
        t.wait_ge(sm.s_prep, 1)
        t.wait_ge(sm.s_xn, 8)
        t.wait_ge(sm.s_lne, 32)
        t.wait_ge(sm.s_b4, 2)
        t.wait_ge(sm.s_vev, 16)
        t.wait_ge(sm.s_sn, 2 * NSTAT)
        for c in range(NE):
            bk_ = c % 4
            if c >= 4:
                t.wait_ge(sm.s_zb, c - 3)
            t.matmul(pbk[bk_][:, :], hdc[:, c, :], xnown[:, c, 0:512], start=True, stop=True)
            t.matmul(pbk[4 + bk_][:, :], hdc[:, c, :], xnown[:, c, 512:1024],
                     start=True, stop=True).then_inc(sm.s_zm, 1)
        LAG = 3
        sb3 = [pbk[0], pbk[1], pbk[6], pbk[7]]
        t.wait_ge(sm.s_zb, 16)
        t.wait_ge(sm.s_b4, 2)
        for j in range(NJ3 + LAG):
            if j < NJ3:
                tau, h, js, sl, n = jobs3[j]
                if js == 0:
                    t.wait_ge([sm.s_qr0, sm.s_qr1][n % 2], 16 * (n // 2 + 1))
                if j >= 4:
                    t.wait_ge(sm.s_px, j - 3)
                t.matmul(sb3[j % 4][:, :], Kb[:, h // 4, sl * 128:(sl + 1) * 128], qd[n % 2][:, :],
                         start=True, stop=True).then_inc(sm.s_sc, 1)
            if j >= LAG:
                jj = j - LAG
                tau, h, js, sl, n = jobs3[jj]
                p = jj % 3
                t.wait_ge(sm.s_pm, jj + 1)
                first = js == 0
                last = jj == grp_last[n]
                if first and n >= 2:
                    t.wait_ge(sm.s_ao, n - 1)
                t.matmul(pbk[2 + n % 2][:, :], Vb[:, h // 4, sl, :], pmb[p][:, :],
                         start=first, stop=last)
                t.matmul(pbk[4 + n % 2][0:1, :], onec[:, :], pmb[p][:, :],
                         start=first, stop=last).then_inc(sm.s_av, 1)
        t.wait_ge(sm.s_px, NJ3)
        t.wait_ge(sm.s_ao, 32)
        t.wait_ge(sm.s_prep, 2)
        for ic in range(NI):
            p = ic % 2
            bu = [pbk[0], pbk[1]] if ic % 2 == 0 else [pbk[4], pbk[5]]
            bg = [pbk[2], pbk[3]] if ic % 2 == 0 else [pbk[6], pbk[7]]
            t.wait_ge([sm.s_u0, sm.s_u1][p], 16 * (ic // 2 + 1))
            if ic >= 2:
                t.wait_ge(sm.s_h, ic - 1)
            for ec in range(NE):
                for tau in range(2):
                    inst = t.matmul(bu[tau][:, :], wub[p][:, ec, :],
                                    zsl[ec][:, tau * 512:(tau + 1) * 512],
                                    start=(ec == 0), stop=(ec == NE - 1))
            inst.then_inc(sm.s_uc, 1)
            t.wait_ge([sm.s_g0, sm.s_g1][p], 16 * (ic // 2 + 1))
            if ic >= 2:
                t.wait_ge(sm.s_sl, 2 * (ic - 1))
            for ec in range(NE):
                for tau in range(2):
                    inst = t.matmul(bg[tau][:, :], wgb[p][:, ec, :],
                                    zsl[ec][:, tau * 512:(tau + 1) * 512],
                                    start=(ec == 0), stop=(ec == NE - 1))
            inst.then_inc(sm.s_gc, 1)
        t.wait_ge(sm.s_sl, 2 * NI)
        for k, c0 in enumerate(p5pairs):
            pp = k % 2
            t.wait_ge(sm.s_h4, max(ord4[c0], ord4[c0 + 32]))
            if k >= 2:
                t.wait_ge(sm.s_m5e, 2 * (k - 2) + 2)
            sA, sB = hperm[c0], hperm[c0 + 32]
            for half in range(2):
                hs = slice(half * 512, half * 512 + 512)
                t.matmul(pbk[4 * pp + half][:, :], h8192[:, :], hsl[sA][:, hs],
                         start=True, stop=False)
                t.matmul(pbk[4 * pp + half][:, :], h8192[:, :], hsl[sB][:, hs],
                         start=False, stop=True)
            for half in range(2):
                hs = slice(half * 512, half * 512 + 512)
                t.matmul(pbk[4 * pp + 2 + half][:, :], h8192[:, :], hsl[sA][:, hs],
                         start=True, stop=False)
                inst = t.matmul(pbk[4 * pp + 2 + half][:, :], h8192n[:, :], hsl[sB][:, hs],
                                start=False, stop=True)
            inst.then_inc(sm.s_m5, 1)
        t.wait_ge(sm.s_m5e, NI)
        t.wait_ge(sm.s_lw, 16 * 32 + 1)
        for qb in range(8):
            if qb >= 2:
                t.wait_ge(sm.s_sf, qb - 1)
            t.transpose(pbk[6 + qb % 2][:, 0:32],
                        lrr[:, (qb % 4) * 128:(qb % 4) * 128 + 128],
                        identf[0:32, 0:32]).then_inc(sm.s_ft, 1)
        def unit_transposes(n):
            t.wait_ge(sm.s_sf, 8 + 3 * n + 1)
            t.wait_ge(sm.s_sf, 8 + 3 * (n - 2) + 2 if n >= 2 else 8)
            dstb = pbk[4 + n % 2].bitcast(BF16)
            for i in range(4):
                inst = t.transpose(dstb[:, i * 128:(i + 1) * 128],
                                   sfst[:, i * 128:(i + 1) * 128], ident[:, :])
            inst.then_inc(sm.s_ft, 1)
            t.wait_ge([sm.s_ar0, sm.s_ar1][n % 2], 16 * (n // 2 + 1))
            t.wait_ge(sm.s_sf, 8 + 3 * (n - 2) + 3 if n >= 2 else 8)
            dstb = pbk[6 + n % 2].bitcast(BF16)
            for i in range(4):
                inst = t.transpose(dstb[:, i * 128:(i + 1) * 128],
                                   attn_s[n % 2][:, i * 128:(i + 1) * 128], ident[:, :])
            inst.then_inc(sm.s_ft, 1)

        for tau in range(2):
            for ec in range(NE):
                n = tau * 16 + ec
                bank = pbk[ec % 4]
                if n >= 4:
                    t.wait_ge(sm.s_sf, 8 + 3 * (n - 4) + 1)
                for u in range(4):
                    un = tau * NQ4 + ec * 4 + u
                    p = un % 2
                    t.wait_ge([sm.s_d0, sm.s_d1][p], 16 * (un // 2 + 1))
                    for gc in range(16):
                        gg = u * 16 + gc
                        inst = t.matmul(bank[:, :], wdb[p][:, gc, :],
                                        hsl[hperm[gg]][:, tau * 512:(tau + 1) * 512],
                                        start=(gg == 0), stop=(gg == NI - 1))
                    inst.then_inc(sm.s_dc, 1)
                if n >= 1:
                    unit_transposes(n - 1)
        unit_transposes(31)

    es.close()
    return nc



def _prep_shared(inp):
    f32 = np.float32
    wq = np.ascontiguousarray(
        np.asarray(inp["L0_attn_Wq"]).reshape(H, NE, 128, 128).transpose(0, 2, 1, 3), dtype=f32)
    wk = np.ascontiguousarray(
        np.asarray(inp["L0_attn_Wk"]).reshape(G, NE, 128, 128).transpose(0, 2, 1, 3), dtype=f32)
    wv = np.ascontiguousarray(
        np.asarray(inp["L0_attn_Wv"]).reshape(G, NE, 128, 128).transpose(0, 2, 1, 3), dtype=f32)
    bq = np.ascontiguousarray(np.asarray(inp["L0_attn_bq"]).T, dtype=f32)
    bk = np.ascontiguousarray(np.asarray(inp["L0_attn_bk"]).T, dtype=f32)
    bv = np.ascontiguousarray(np.asarray(inp["L0_attn_bv"]).T, dtype=f32)
    wup = np.ascontiguousarray(
        np.asarray(inp["L0_ffn_W_up"]).T.reshape(NE, 128, NI, 128).transpose(2, 1, 0, 3), dtype=f32)
    wg = np.ascontiguousarray(
        np.asarray(inp["L0_ffn_W_gate"]).T.reshape(NE, 128, NI, 128).transpose(2, 1, 0, 3), dtype=f32)
    wd = np.ascontiguousarray(
        np.asarray(inp["L0_ffn_W_down"]).T.reshape(NI, 128, NE, 128).transpose(2, 1, 0, 3), dtype=f32)
    suug = np.ascontiguousarray(np.asarray(inp["L0_ffn_SU_ug"]).reshape(NE, 128).T, dtype=f32)
    sudp = np.ascontiguousarray(np.asarray(inp["L0_ffn_SU_dp"]).reshape(NI, 128).T, dtype=f32)
    wsug = np.asarray(inp["L0_ffn_Wscale_ug"])
    wsup = np.ascontiguousarray(wsug[:INTER].reshape(NI, 128).T, dtype=f32)
    wsg = np.ascontiguousarray(wsug[INTER:].reshape(NI, 128).T, dtype=f32)
    wsdp = np.ascontiguousarray(np.asarray(inp["L0_ffn_Wscale_dp"]).reshape(NE, 128).T, dtype=f32)
    return dict(wq=wq, wk=wk, wv=wv, bq=bq, bk=bk, bv=bv, wup=wup, wgate=wg,
                wdown=wd, suug=suug, sudp=sudp, wsup=wsup, wsg=wsg, wsdp=wsdp)


def _quarters(parity):
    return [0, 3, 1, 2] if parity == 0 else [1, 2, 0, 3]


def _masks_for(parity):
    qmap = _quarters(parity)
    out = np.zeros((24, 128, 512), dtype=np.float32)
    j = 0
    for tau in (0, 1):
        oq = qmap[tau] * 512
        for sl in SCHED[tau]:
            c_abs = qmap[sl // 4] * 4 + sl % 4
            k_abs = c_abs * 128 + np.arange(128)[:, None]
            q_abs = oq + np.arange(512)[None, :]
            out[j] = (k_abs <= q_abs).astype(np.float32)
            j += 1
    return np.ascontiguousarray(out.transpose(1, 0, 2)).astype(BF)


_NC_CACHE = {}


def kernel(**inputs):
    shared = _prep_shared(inputs)
    x = np.asarray(inputs["L0_attn_x"], dtype=np.float32)
    masks = [_masks_for(0), _masks_for(1)]

    if "nc" not in _NC_CACHE:
        _NC_CACHE["nc"] = build_nc()
    nc = _NC_CACHE["nc"]

    in_maps = []
    for core in range(8):
        b = core // 2
        par = core % 2
        qmap = _quarters(par)
        x4 = np.ascontiguousarray(
            np.stack([x[b, q * 512:(q + 1) * 512, :] for q in qmap]))
        m = dict(shared)
        m["x4"] = x4
        m["masks"] = masks[par]
        in_maps.append(m)

    res = run_bass_kernel_spmd(nc, in_maps, core_ids=list(range(8)))

    out = np.empty((B, S, E), dtype=np.float32)
    for core in range(8):
        b = core // 2
        qmap = _quarters(core % 2)
        oarr = res.results[core]["out"]
        out[b, qmap[0] * 512:(qmap[0] + 1) * 512, :] = oarr[0:512]
        out[b, qmap[1] * 512:(qmap[1] + 1) * 512, :] = oarr[512:1024]
    return out
